# version 04
# baseline (speedup 1.0000x reference)
import numpy as np
from contextlib import ExitStack

import concourse.bass as bass
import concourse.mybir as mybir
from concourse.bass_utils import run_bass_kernel_spmd

F32 = mybir.dt.float32
AF = mybir.ActivationFunctionType
ALU = mybir.AluOpType
AXL = mybir.AxisListType

B, L, D = 8, 4096, 1024
H, DH = 8, 128
C = 128
NCH = L // C
E, NS, HID, ES = 16, 16, 256, 256
LN_EPS = 1e-5
NORM_EPS = 1e-12
ROPE_BASE = 10000.0

_STAGE = 99
_S6 = 9
_DO_B = True
_DO_C = True
_NCHA = None

_BSPAN = 0.999 - 0.85
_D_HI = float(np.float32(_BSPAN)); _D_LO = float(np.float32(_BSPAN - _D_HI))
_B_HI = float(np.float32(0.85));   _B_LO = float(np.float32(0.85 - _B_HI))


class _Res:
    __slots__ = ("w", "r")

    def __init__(self):
        self.w = None
        self.r = []


class _Seg:
    def __init__(self, h, off, width):
        self.h, self.off, self.wd = h, off, width

    def __call__(self, a=0, b=None):
        b = self.wd if b is None else b
        return self.h[:, self.off + a:self.off + b]


class _P:
    ESEM = {"t": "sT", "v": "sV", "s": "sS"}
    QSEM = {"y": "dY", "g": "dG"}

    def __init__(self):
        self.stream = {e: [] for e in "tvsgy"}
        self.cnt = {n: 0 for n in ("sT", "sV", "sS", "dY", "dG")}
        self.seen = {}
        self.sems = None

    def wait(self, e, tok):
        if tok is None:
            return
        s, v = tok
        if self.seen.get((e, s), 0) >= v:
            return
        self.seen[(e, s)] = v
        self.stream[e].append(lambda eng, s=s, v=v: eng.wait_ge(self.sems[s], v))

    def waits(self, e, toks):
        for t in toks:
            self.wait(e, t)

    def use(self, e, *ress):
        for r in ress:
            self.wait(e, r.w)

    def claim(self, e, *ress):
        for r in ress:
            for t in r.r:
                self.wait(e, t)
            r.r = []

    def op(self, e, fn):
        self.stream[e].append(fn)

    def dinc(self, e):
        s = self.ESEM[e]
        self.cnt[s] += 1
        v = self.cnt[s]
        self.stream[e].append(lambda eng, s=s: eng.drain().then_inc(self.sems[s], 1))
        return (s, v)

    def dma(self, e, out, in_):
        s = self.QSEM[e]
        self.cnt[s] += 16
        v = self.cnt[s]
        self.stream[e].append(
            lambda eng, o=out, i=in_: eng.dma_start(out=o, in_=i).then_inc(self.sems[s], 16))
        return (s, v)


def _build(taps=()):
    taps = set(taps)
    nc = bass.Bass("TRN2", target_bir_lowering=False, detect_race_conditions=False)

    dt = nc.dram_tensor
    x_h = dt("x", [L, D], F32, kind="ExternalInput").ap()
    wq_h = dt("wqT", [D, D], F32, kind="ExternalInput").ap()
    wk_h = dt("wkT", [D, D], F32, kind="ExternalInput").ap()
    wv_h = dt("wvT", [D, D], F32, kind="ExternalInput").ap()
    wo_h = dt("woT", [D, D], F32, kind="ExternalInput").ap()
    wab_h = dt("wabT", [D, 16], F32, kind="ExternalInput").ap()
    bo_h = dt("boflat", [1, D], F32, kind="ExternalInput").ap()
    mu_h = dt("muflat", [D, ES], F32, kind="ExternalInput").ap()
    gbg_h = dt("gbg", [128, D], F32, kind="ExternalInput").ap()
    gbb_h = dt("gbb", [128, D], F32, kind="ExternalInput").ap()
    scl_h = dt("sclbc", [128, 1], F32, kind="ExternalInput").ap()
    b1c_h = dt("b1cols", [128, 32], F32, kind="ExternalInput").ap()
    b2r_h = dt("b2rep", [128, 2 * D], F32, kind="ExternalInput").ap()
    w1_h = dt("w1", [E, D, HID], F32, kind="ExternalInput").ap()
    w2_h = dt("w2", [E, HID, D], F32, kind="ExternalInput").ap()
    cos_h = dt("costab", [L, DH], F32, kind="ExternalInput").ap()
    sin_h = dt("sintab", [L, DH], F32, kind="ExternalInput").ap()
    idn_h = dt("identity", [128, 128], F32, kind="ExternalInput").ap()
    msk_h = dt("maskut", [128, 128], F32, kind="ExternalInput").ap()
    oc_h = dt("onescol", [128, 1], F32, kind="ExternalInput").ap()
    o8_h = dt("ones8", [H, 128], F32, kind="ExternalInput").ap()
    cst_h = dt("cst", [128, 4], F32, kind="ExternalInput").ap()

    x2_hbm = dt("x2scr", [L, D], F32).ap()
    lg_hbm = dt("lgscr", [2, 128, L], F32).ap()
    y_h = dt("y", [L, D], F32, kind="ExternalOutput").ap()

    with ExitStack() as stk:
        def sb(name, shape):
            return stk.enter_context(nc.sbuf_tensor(name, shape, F32))

        wq = sb("wq", [128, 8 * D]); wk = sb("wk", [128, 8 * D])
        wv = sb("wv", [128, 8 * D]); wo = sb("wo", [128, 8 * D])
        wab = sb("wab", [128, 8 * 16])
        mu = sb("mu", [128, 8 * ES])
        idn = sb("idn", [128, 128]); msk = sb("msk", [128, 128])
        ocol = sb("ocol", [128, 1])
        o8 = sb("o8s", [H, 128])
        orow = o8[0:1, 0:128]
        b1c = sb("b1cs", [128, 32])
        scl = sb("scls", [128, 1]); cst = sb("csts", [128, 4])

        poolA = sb("poolA", [128, 4096])
        qtm = _Seg(poolA, 0, 1024)
        ktm = _Seg(poolA, 1024, 1024)
        vtm = _Seg(poolA, 2048, 1024)
        gbg = _Seg(poolA, 3072, 1024)
        w1b = [_Seg(poolA, 0, 2048), _Seg(poolA, 2048, 2048)]

        poolB = sb("poolB", [128, 4096])
        qhT = _Seg(poolB, 0, 1024)
        khT = _Seg(poolB, 1024, 1024)
        xn1 = _Seg(poolB, 2048, 1024)
        gbb = _Seg(poolB, 3072, 1024)
        w2b = [_Seg(poolB, 0, 2048), _Seg(poolB, 2048, 2048)]
        x2c = [_Seg(poolB, 0, 1024), _Seg(poolB, 1024, 1024)]
        outc = [_Seg(poolB, 2048, 1024), _Seg(poolB, 3072, 1024)]

        poolC = sb("poolC", [128, 2048])
        xnT = _Seg(poolC, 0, 1024)
        siT = _Seg(poolC, 0, 2048)
        ltm = _Seg(poolC, 0, 256)
        ptc = _Seg(poolC, 256, 256)
        ct = _Seg(poolC, 512, 256)
        lgc = [_Seg(poolC, 1024, 256), _Seg(poolC, 1280, 256)]

        xb = [sb("xb0", [128, D]), sb("xb1", [128, D])]
        csb = sb("csb", [128, 256])
        x2b = sb("x2b", [128, D])
        swp = _Seg(x2b, 0, 1024)
        khtm = swp
        am = [sb("am0", [128, 128]), sb("am1", [128, 128])]
        lgsb = sb("lgsb", [128, 256])
        psb = sb("psb", [128, 128])
        aacc = sb("aacc", [128, 2 * D])
        so = [_Seg(aacc, 0, 1024), _Seg(aacc, 1024, 1024)]
        s8 = sb("s8", [128, 8 * 128])
        fm = sb("fm", [128, 2]); fz = sb("fz", [128, 2])
        sg16 = sb("sg16", [8, 256])
        smt = sb("smt", [128, 96])
        witm = smt[:, 0:8]; dtm = smt[:, 8:16]; wstm = smt[:, 16:24]
        dbc = smt[:, 24:32]; sm1 = smt[:, 32:40]; st6 = smt[:, 40:52]
        mv = smt[:, 52:54]; rstd = smt[:, 54:55]; dlT = smt[0:1, 56:64]
        hts = smt[:, 64:96]

        PT = [stk.enter_context(nc.psum_tensor("p%d" % i, [128, 1024], F32))
              for i in range(4)]

        bb1 = am[0][0:8, 0:128]
        bb2 = am[1][0:8, 0:128]
        dhm = psb[0:8, 0:128]
        wih = am[0][0:8, 0:128]
        wsh = am[1][0:8, 0:128]
        mun = lgsb[0:1, 0:256]
        bo = poolC[0:1, 1024:2048]

        sems = {n: stk.enter_context(nc.semaphore(n))
                for n in ("sT", "sV", "sS", "dY", "dG")}

        p = _P()
        p.sems = sems

        rPS = {(i, hf): _Res() for i in range(4) for hf in range(2)}
        rXB = [_Res(), _Res()]; rCS = _Res(); rX2 = _Res()
        rXN1 = _Res(); rXNT = _Res(); rQ = _Res(); rK = _Res(); rKH = _Res()
        rV = _Res(); rQHT = _Res(); rKHT = _Res(); rAM = [_Res(), _Res()]
        rSG = _Res(); rAB = _Res(); rWI = _Res(); rDBC = _Res()
        rS8 = _Res(); rAACC = _Res(); rFMZ = _Res(); rLG = _Res(); rPSB = _Res()
        rPTM = _Res(); rMU = _Res(); rSTAT = _Res()
        rW1B = [_Res(), _Res()]; rW2B = [_Res(), _Res()]; rHTS = _Res()
        rSOT = _Res(); rSIT = _Res()
        rLGC = [_Res(), _Res()]; rX2C = [_Res(), _Res()]; rOUTC = [_Res(), _Res()]
        rLTM = _Res(); rPTC = _Res(); rCT = _Res()

        def PA(i, half, a=0, b=512, pp0=0, pp1=128):
            return PT[i][pp0:pp1, half * 512 + a: half * 512 + b]

        def tap(name, ap, tok, shape, res=None):
            if name not in taps:
                return
            out = nc.dram_tensor("tap_" + name, list(shape), F32,
                                 kind="ExternalOutput").ap()
            p.wait("g", tok)
            dtok = p.dma("g", out, ap)
            if res is not None:
                res.r.append(dtok)

        for dst, src in (
            (wq[:, :].rearrange("p (k n) -> p k n", k=8),
             wq_h.rearrange("(k p) n -> p k n", p=128)),
            (wk[:, :].rearrange("p (k n) -> p k n", k=8),
             wk_h.rearrange("(k p) n -> p k n", p=128)),
            (wv[:, :].rearrange("p (k n) -> p k n", k=8),
             wv_h.rearrange("(k p) n -> p k n", p=128)),
            (wo[:, :].rearrange("p (k n) -> p k n", k=8),
             wo_h.rearrange("(k p) n -> p k n", p=128)),
            (wab[:, :].rearrange("p (k n) -> p k n", k=8),
             wab_h.rearrange("(k p) n -> p k n", p=128)),
            (mu[:, :].rearrange("p (k n) -> p k n", k=8),
             mu_h.rearrange("(k p) n -> p k n", p=128)),
            (gbg(), gbg_h), (gbb(), gbb_h),
            (idn[:, :], idn_h), (msk[:, :], msk_h),
            (ocol[:, :], oc_h),
            (o8[:, :], o8_h),
            (bo, bo_h),
            (b1c[:, :], b1c_h),
            (scl[:, :], scl_h), (cst[:, :], cst_h),
        ):
            tok_lds = p.dma("y", dst, src)

        p.wait("v", tok_lds)
        p.op("v", lambda eng: eng.memset(s8[:, :], 0.0))
        p.op("v", lambda eng: eng.memset(fm[:, :], -1e30))
        p.op("v", lambda eng: eng.memset(fz[:, :], 0.0))
        p.op("v", lambda eng: eng.memset(aacc[:, :], 0.0))
        t0 = p.dinc("v")
        rS8.w = t0; rFMZ.w = t0; rAACC.w = t0

        p.wait("s", tok_lds)
        p.op("s", lambda eng: eng.activation(out=xn1(), in_=mu[:, 0:1024],
                                             func=AF.Square))
        p.op("s", lambda eng: eng.activation(out=swp(), in_=mu[:, 1024:2048],
                                             func=AF.Square))
        tmsq = p.dinc("s")
        p.wait("t", tmsq)
        for k in range(8):
            src = xn1(k * 256, (k + 1) * 256) if k < 4 else \
                swp((k - 4) * 256, (k - 3) * 256)
            p.op("t", lambda eng, s=src, k=k: eng.matmul(
                PA(0, 0, 0, 256, 0, 1), ocol[:, :], s, start=(k == 0), stop=(k == 7)))
        tcs = p.dinc("t")
        rPS[(0, 0)].w = tcs
        rXN1.r.append(tcs)
        p.wait("s", tcs)
        p.op("s", lambda eng: eng.activation(out=mun, in_=PA(0, 0, 0, 256, 0, 1),
                                             func=AF.Sqrt))
        tsq = p.dinc("s")
        rPS[(0, 0)].r.append(tsq)
        p.wait("v", tsq)
        p.op("v", lambda eng: eng.tensor_scalar(out=mun, in0=mun,
                                                scalar1=NORM_EPS, scalar2=None,
                                                op0=ALU.max))
        p.op("v", lambda eng: eng.reciprocal(mun, mun))
        trn = p.dinc("v")
        p.wait("t", trn)
        p.claim("t", rPS[(0, 0)])
        p.op("t", lambda eng: eng.matmul(PA(0, 0, 0, 256, 0, 128), orow,
                                         mun, start=True, stop=True))
        tbc = p.dinc("t")
        rPS[(0, 0)].w = tbc
        p.wait("v", tbc)
        for k in range(8):
            p.op("v", lambda eng, k=k: eng.tensor_tensor(
                out=mu[:, k * 256:(k + 1) * 256], in0=mu[:, k * 256:(k + 1) * 256],
                in1=PA(0, 0, 0, 256), op=ALU.mult))
        tmu = p.dinc("v")
        rMU.w = tmu
        rPS[(0, 0)].r.append(tmu)

        def chunk_A(i):
            sl = i % 2
            p.claim("y", rXB[sl])
            rXB[sl].w = p.dma("y", xb[sl][:, :], x_h[i * C:(i + 1) * C, :])
            p.claim("y", rCS)
            p.dma("y", csb[:, 0:128], cos_h[i * C:(i + 1) * C, :])
            rCS.w = p.dma("y", csb[:, 128:256], sin_h[i * C:(i + 1) * C, :])

            p.use("v", rXB[sl]); p.claim("v", rSTAT)
            p.op("v", lambda eng: eng.bn_stats(smt[:, 40:46], xb[sl][:, 0:512]))
            p.op("v", lambda eng: eng.bn_stats(smt[:, 46:52], xb[sl][:, 512:1024]))
            p.op("v", lambda eng: eng.drain())
            p.op("v", lambda eng: eng.bn_aggr(smt[:, 52:54], smt[:, 40:52]))
            tv = p.dinc("v")
            p.wait("s", tv)
            p.op("s", lambda eng: eng.activation(out=smt[:, 54:55], in_=smt[:, 53:54],
                                                 func=AF.Sqrt, bias=cst[:, 0:1],
                                                 scale=1.0))
            ts_ = p.dinc("s")
            p.wait("v", ts_)
            p.op("v", lambda eng: eng.reciprocal(smt[:, 54:55], smt[:, 54:55]))
            p.op("v", lambda eng: eng.drain())
            p.claim("v", rXN1)
            p.op("v", lambda eng: eng.tensor_scalar(out=xn1(), in0=xb[sl][:, :],
                                                    scalar1=smt[:, 52:53],
                                                    scalar2=smt[:, 54:55],
                                                    op0=ALU.subtract, op1=ALU.mult))
            rXN1.w = p.dinc("v")
            rSTAT.w = rXN1.w
            tap("xb_%d" % i, xb[sl][:, :], rXN1.w, (128, 1024), rXB[sl])
            tap("mv_%d" % i, smt[:, 52:54], rXN1.w, (128, 2), rSTAT)
            tap("xn1_%d" % i, xn1(), rXN1.w, (128, 1024), rXN1)

            if _STAGE < 2:
                return
            p.use("t", rXN1)
            p.claim("t", rPS[(3, 0)], rPS[(3, 1)])
            for j in range(8):
                half, qtr = (0, j) if j < 4 else (1, j - 4)
                p.op("t", lambda eng, j=j, half=half, qtr=qtr: eng.transpose(
                    PA(3, half, qtr * 128, (qtr + 1) * 128),
                    xn1(j * 128, (j + 1) * 128), idn[:, :]))
            tt = p.dinc("t")
            rPS[(3, 0)].w = tt; rPS[(3, 1)].w = tt
            p.wait("s", tt)
            p.claim("s", rXNT)
            p.op("s", lambda eng: eng.copy(xnT(0, 512), PA(3, 0)))
            p.op("s", lambda eng: eng.copy(xnT(512, 1024), PA(3, 1)))
            rXNT.w = p.dinc("s")
            rPS[(3, 0)].r.append(rXNT.w); rPS[(3, 1)].r.append(rXNT.w)
            rXN1.r.append(rXNT.w)

            if _STAGE < 3:
                return
            p.use("t", rXNT)
            p.claim("t", rPS[(0, 0)], rPS[(0, 1)], rPS[(1, 0)], rPS[(1, 1)],
                    rPS[(2, 0)], rPS[(2, 1)], rPS[(3, 0)], rPS[(3, 1)])
            for k in range(8):
                for (pi, w_) in ((0, wq), (1, wk), (2, wv)):
                    for half in range(2):
                        p.op("t", lambda eng, pi=pi, w_=w_, half=half, k=k:
                             eng.matmul(PA(pi, half), xnT(k * 128, (k + 1) * 128),
                                        w_[:, k * 1024 + half * 512:
                                           k * 1024 + (half + 1) * 512],
                                        start=(k == 0), stop=(k == 7)))
                p.op("t", lambda eng, k=k: eng.matmul(
                    PA(3, 0, 0, 128, 0, 8), wab[:, k * 16:k * 16 + 8],
                    xnT(k * 128, (k + 1) * 128), start=(k == 0), stop=(k == 7)))
                p.op("t", lambda eng, k=k: eng.matmul(
                    PA(3, 1, 0, 128, 0, 8), wab[:, k * 16 + 8:(k + 1) * 16],
                    xnT(k * 128, (k + 1) * 128), start=(k == 0), stop=(k == 7)))
            tproj = p.dinc("t")
            for key in ((0, 0), (0, 1), (1, 0), (1, 1), (2, 0), (2, 1),
                        (3, 0), (3, 1)):
                rPS[key].w = tproj

            if _STAGE < 4:
                return
            p.wait("s", tproj)
            p.claim("s", rSG, rV)
            p.op("s", lambda eng: eng.activation(out=sg16[0:8, 0:128],
                                                 in_=PA(3, 0, 0, 128, 0, 8),
                                                 func=AF.Sigmoid, bias=cst[0:8, 1:2],
                                                 scale=1.0))
            p.op("s", lambda eng: eng.activation(out=sg16[0:8, 128:256],
                                                 in_=PA(3, 1, 0, 128, 0, 8),
                                                 func=AF.Sigmoid, bias=cst[0:8, 2:3],
                                                 scale=1.0))
            p.op("s", lambda eng: eng.copy(vtm(0, 512), PA(2, 0)))
            p.op("s", lambda eng: eng.copy(vtm(512, 1024), PA(2, 1)))
            tvv = p.dinc("s")
            rSG.w = tvv; rV.w = tvv
            rPS[(3, 0)].r.append(tvv); rPS[(3, 1)].r.append(tvv)
            rPS[(2, 0)].r.append(tvv); rPS[(2, 1)].r.append(tvv)
            tap("vtm_%d" % i, vtm(), tvv, (128, 1024), rV)

            if _STAGE < 5:
                return
            p.use("v", rSG); p.claim("v", rAB)
            p.op("v", lambda eng: eng.tensor_scalar(out=bb1, in0=sg16[0:8, 0:128],
                                                    scalar1=_D_HI, scalar2=_B_HI,
                                                    op0=ALU.mult, op1=ALU.add))
            p.op("v", lambda eng: eng.tensor_scalar(out=bb2, in0=sg16[0:8, 0:128],
                                                    scalar1=_D_LO, scalar2=_B_LO,
                                                    op0=ALU.mult, op1=ALU.add))
            p.op("v", lambda eng: eng.tensor_tensor(out=bb1, in0=bb1,
                                                    in1=bb2, op=ALU.add))
            tb = p.dinc("v")
            p.wait("s", tb)
            p.op("s", lambda eng: eng.activation(out=bb2, in_=bb1,
                                                 func=AF.Ln))
            tl = p.dinc("s")
            p.wait("v", tl)
            p.op("v", lambda eng: eng.tensor_tensor_scan(dhm, o8[:, :],
                                                         bb2, 0.0,
                                                         ALU.mult, ALU.add))
            tc = p.dinc("v")
            p.wait("s", tc)
            p.op("s", lambda eng: eng.activation(out=dhm, in_=dhm,
                                                 func=AF.Exp))
            td = p.dinc("s")
            p.wait("v", td)
            p.op("v", lambda eng: eng.reciprocal(wih, dhm))
            p.op("v", lambda eng: eng.drain())
            p.op("v", lambda eng: eng.tensor_tensor(out=wih, in0=sg16[0:8, 128:256],
                                                    in1=wih, op=ALU.mult))
            p.op("v", lambda eng: eng.tensor_scalar(out=wsh, in0=wih,
                                                    scalar1=psb[0:8, 127:128],
                                                    scalar2=None, op0=ALU.mult))
            tw = p.dinc("v")
            rAB.w = tw
            rSG.r.append(tw)
            tap("dhm_%d" % i, dhm, tw, (8, 128), rAB)
            tap("wih_%d" % i, wih, tw, (8, 128), rAM[0])

            if _STAGE < 6:
                return
            p.wait("t", tw)
            p.claim("t", rPS[(3, 1)])
            if _S6 >= 1:
                p.op("t", lambda eng: eng.transpose(PA(3, 1, 0, 8), wih,
                                                    idn[0:8, 0:8]))
            if _S6 >= 2:
                p.op("t", lambda eng: eng.transpose(PA(3, 1, 8, 16), dhm,
                                                    idn[0:8, 0:8]))
            if _S6 >= 3:
                p.op("t", lambda eng: eng.transpose(PA(3, 1, 16, 24), wsh,
                                                    idn[0:8, 0:8]))
            if _S6 >= 4:
                p.op("t", lambda eng: eng.transpose(PA(3, 1, 24, 32, 0, 1),
                                                    psb[0:8, 127:128], idn[0:8, 0:8]))
            t1 = p.dinc("t")
            rPS[(3, 1)].w = t1
            p.wait("s", t1)
            p.claim("s", rWI)
            if _S6 >= 1:
                p.op("s", lambda eng: eng.copy(witm, PA(3, 1, 0, 8)))
            if _S6 >= 2:
                p.op("s", lambda eng: eng.copy(dtm, PA(3, 1, 8, 16)))
            if _S6 >= 3:
                p.op("s", lambda eng: eng.copy(wstm, PA(3, 1, 16, 24)))
            if _S6 >= 4:
                p.op("s", lambda eng: eng.copy(dlT, PA(3, 1, 24, 32, 0, 1)))
            t2 = p.dinc("s")
            rWI.w = t2
            p.wait("t", t2)
            if _S6 >= 5:
                p.op("t", lambda eng: eng.matmul(PA(3, 1, 32, 40), orow, dlT,
                                                 start=True, stop=True))
            t3 = p.dinc("t")
            p.wait("s", t3)
            p.claim("s", rDBC)
            if _S6 >= 6:
                p.op("s", lambda eng: eng.copy(dbc, PA(3, 1, 32, 40)))
            t4 = p.dinc("s")
            rDBC.w = t4
            rPS[(3, 1)].r.append(t4)

            if _STAGE < 7:
                return
            cosb = bass.AP(csb, 0, [[256, 128], [0, 8], [1, 128]])
            sinb = bass.AP(csb, 128, [[256, 128], [0, 8], [1, 128]])
            swp_ev = bass.AP(x2b, 0, [[1024, 128], [2, 512]])
            swp_od = bass.AP(x2b, 1, [[1024, 128], [2, 512]])
            p.waits("v", [tproj, t2]); p.use("v", rCS)
            p.claim("v", rQ, rX2)
            p.op("v", lambda eng: eng.tensor_copy(
                swp_od, bass.AP(PT[0], 0, [[1024, 128], [2, 512]])))
            p.op("v", lambda eng: eng.tensor_copy(
                swp_ev, bass.AP(PT[0], 1, [[1024, 128], [2, 512]])))
            p.op("v", lambda eng: eng.tensor_tensor(out=qtm(), in0=PT[0][:, :],
                                                    in1=cosb, op=ALU.mult))
            p.op("v", lambda eng: eng.tensor_tensor(out=swp(), in0=swp(), in1=sinb,
                                                    op=ALU.mult))
            p.op("v", lambda eng: eng.tensor_tensor(out=qtm(), in0=qtm(), in1=swp(),
                                                    op=ALU.add))
            for h in range(H):
                p.op("v", lambda eng, h=h: eng.tensor_scalar(
                    out=qtm(h * 128, (h + 1) * 128), in0=qtm(h * 128, (h + 1) * 128),
                    scalar1=smt[:, 8 + h:9 + h], scalar2=None, op0=ALU.mult))
            rQ.w = p.dinc("v")
            rPS[(0, 0)].r.append(rQ.w); rPS[(0, 1)].r.append(rQ.w)
            rXN1.r.append(rQ.w)
            tap("qhat_%d" % i, qtm(), rQ.w, (128, 1024), rQ)

            if _STAGE < 8:
                return
            p.claim("v", rK, rKH)
            p.op("v", lambda eng: eng.tensor_copy(
                swp_od, bass.AP(PT[1], 0, [[1024, 128], [2, 512]])))
            p.op("v", lambda eng: eng.tensor_copy(
                swp_ev, bass.AP(PT[1], 1, [[1024, 128], [2, 512]])))
            p.op("v", lambda eng: eng.tensor_tensor(out=ktm(), in0=PT[1][:, :],
                                                    in1=cosb, op=ALU.mult))
            p.op("v", lambda eng: eng.tensor_tensor(out=swp(), in0=swp(), in1=sinb,
                                                    op=ALU.mult))
            p.op("v", lambda eng: eng.tensor_tensor(out=ktm(), in0=ktm(), in1=swp(),
                                                    op=ALU.add))
            for h in range(H):
                p.op("v", lambda eng, h=h: eng.tensor_scalar(
                    out=khtm(h * 128, (h + 1) * 128), in0=ktm(h * 128, (h + 1) * 128),
                    scalar1=smt[:, 16 + h:17 + h], scalar2=None, op0=ALU.mult))
            tk = p.dinc("v")
            rK.w = tk; rKH.w = tk
            rPS[(1, 0)].r.append(tk); rPS[(1, 1)].r.append(tk)
            rCS.r.append(tk)
            tap("krot_%d" % i, ktm(), tk, (128, 1024), rK)
            tap("khat_%d" % i, khtm(), tk, (128, 1024), rKH)

            if _STAGE < 9:
                return
            p.use("t", rQ, rK)
            p.claim("t", rPS[(1, 0)], rPS[(1, 1)])
            for h in range(4):
                p.op("t", lambda eng, h=h: eng.transpose(
                    PA(1, 0, h * 128, (h + 1) * 128), qtm(h * 128, (h + 1) * 128),
                    idn[:, :]))
                p.op("t", lambda eng, h=h: eng.transpose(
                    PA(1, 1, h * 128, (h + 1) * 128), ktm(h * 128, (h + 1) * 128),
                    idn[:, :]))
            tg1 = p.dinc("t")
            rPS[(1, 0)].w = tg1; rPS[(1, 1)].w = tg1
            p.wait("s", tg1)
            p.claim("s", rQHT, rKHT)
            p.op("s", lambda eng: eng.copy(qhT(0, 512), PA(1, 0)))
            p.op("s", lambda eng: eng.copy(khT(0, 512), PA(1, 1)))
            te1 = p.dinc("s")
            rPS[(1, 0)].r.append(te1); rPS[(1, 1)].r.append(te1)
            p.claim("t", rPS[(3, 0)], rPS[(3, 1)])
            for h in range(4, 8):
                p.op("t", lambda eng, h=h: eng.transpose(
                    PA(3, 0, (h - 4) * 128, (h - 3) * 128),
                    qtm(h * 128, (h + 1) * 128), idn[:, :]))
                p.op("t", lambda eng, h=h: eng.transpose(
                    PA(3, 1, (h - 4) * 128, (h - 3) * 128),
                    ktm(h * 128, (h + 1) * 128), idn[:, :]))
            tg2 = p.dinc("t")
            rPS[(3, 0)].w = tg2; rPS[(3, 1)].w = tg2
            p.wait("s", tg2)
            p.op("s", lambda eng: eng.copy(qhT(512, 1024), PA(3, 0)))
            p.op("s", lambda eng: eng.copy(khT(512, 1024), PA(3, 1)))
            te2 = p.dinc("s")
            rQHT.w = te2; rKHT.w = te2
            rPS[(3, 0)].r.append(te2); rPS[(3, 1)].r.append(te2)

            if _STAGE < 10:
                return
            tTA = [None] * H
            tVam = [None] * H
            tvs_last = [None]

            def emit_AT(h):
                p.wait("t", te1 if h < 4 else te2)
                bk = (1, h % 2)
                p.claim("t", rPS[bk])
                p.op("t", lambda eng, h=h, bk=bk: eng.matmul(
                    PA(bk[0], bk[1], 0, 128), khT(h * 128, (h + 1) * 128),
                    qhT(h * 128, (h + 1) * 128), start=True, stop=True))
                tTA[h] = p.dinc("t")
                rPS[bk].w = tTA[h]

            def emit_Am(h):
                bk = (1, h % 2)
                p.wait("v", tTA[h]); p.use("v", rWI)
                p.claim("v", rAM[h % 2])
                p.op("v", lambda eng, h=h, bk=bk: eng.tensor_scalar(
                    out=am[h % 2][:, :], in0=PA(bk[0], bk[1], 0, 128),
                    scalar1=smt[:, h:h + 1], scalar2=None, op0=ALU.mult))
                p.op("v", lambda eng, h=h: eng.tensor_tensor(
                    out=am[h % 2][:, :], in0=am[h % 2][:, :], in1=msk[:, :],
                    op=ALU.mult))
                tVam[h] = p.dinc("v")
                rAM[h % 2].w = tVam[h]
                rPS[bk].r.append(tVam[h])

            def emit_oD(h):
                ob = (2, h % 2); db = (0, h % 2)
                p.wait("t", tVam[h]); p.use("t", rV, rKH, rS8)
                p.claim("t", rPS[ob], rPS[db])
                p.op("t", lambda eng, h=h, ob=ob: eng.matmul(
                    PA(ob[0], ob[1], 0, 128), vtm(h * 128, (h + 1) * 128),
                    am[h % 2][:, :], start=True, stop=False))
                p.op("t", lambda eng, h=h, ob=ob: eng.matmul(
                    PA(ob[0], ob[1], 0, 128), s8[:, h * 128:(h + 1) * 128],
                    qhT(h * 128, (h + 1) * 128), start=False, stop=True))
                p.op("t", lambda eng, h=h, db=db: eng.matmul(
                    PA(db[0], db[1], 0, 128), khtm(h * 128, (h + 1) * 128),
                    vtm(h * 128, (h + 1) * 128), start=True, stop=True))
                tod = p.dinc("t")
                rPS[ob].w = tod; rPS[db].w = tod
                rAM[h % 2].r.append(tod)
                p.wait("s", tod)
                p.op("s", lambda eng, h=h, ob=ob: eng.copy(
                    am[h % 2][:, :], PA(ob[0], ob[1], 0, 128)))
                tso = p.dinc("s")
                rPS[ob].r.append(tso)
                rAM[h % 2].w = tso
                p.wait("t", tso)
                for half in range(2):
                    p.op("t", lambda eng, h=h, half=half: eng.matmul(
                        PA(3, half), am[h % 2][:, :],
                        wo[:, h * 1024 + half * 512: h * 1024 + (half + 1) * 512],
                        start=(h == 0), stop=False))
                if h == H - 1:
                    for half in range(2):
                        p.op("t", lambda eng, half=half: eng.matmul(
                            PA(3, half), orow,
                            poolC[0:1, 1024 + half * 512:1024 + (half + 1) * 512],
                            start=False, stop=True))
                tap("oT_%d_%d" % (i, h), am[h % 2][:, :], tso, (128, 128), rAM[h % 2])
                p.wait("v", tod); p.use("v", rDBC)
                p.op("v", lambda eng, h=h, db=db: eng.scalar_tensor_tensor(
                    out=s8[:, h * 128:(h + 1) * 128],
                    in0=s8[:, h * 128:(h + 1) * 128],
                    scalar=smt[:, 24 + h:25 + h], in1=PA(db[0], db[1], 0, 128),
                    op0=ALU.mult, op1=ALU.add))
                tvs = p.dinc("v")
                rPS[db].r.append(tvs)
                tvs_last[0] = tvs

            p.claim("t", rPS[(3, 0)], rPS[(3, 1)])
            emit_AT(0)
            for h in range(H):
                if h + 1 < H:
                    emit_AT(h + 1)
                emit_Am(h)
                emit_oD(h)
            rS8.w = tvs_last[0]
            tap("s8_%d" % i, s8[:, :], rS8.w, (128, 1024), rS8)
            two = p.dinc("t")
            rPS[(3, 0)].w = two; rPS[(3, 1)].w = two
            if _STAGE < 11:
                return
            p.wait("v", two); p.use("v", rXB[sl])
            p.op("v", lambda eng: eng.tensor_tensor(out=x2b[:, :],
                                                    in0=xb[sl][:, :],
                                                    in1=PT[3][:, :], op=ALU.add))
            rX2.w = p.dinc("v")
            rXB[sl].r.append(rX2.w)
            rPS[(3, 0)].r.append(rX2.w); rPS[(3, 1)].r.append(rX2.w)
            tap("x2_%d" % i, x2b[:, :], rX2.w, (128, 1024), rX2)
            p.wait("g", rX2.w)
            rX2.r.append(p.dma("g", x2_hbm[i * C:(i + 1) * C, :], x2b[:, :]))

            if _STAGE < 12:
                return
            p.use("v", rX2)
            p.op("v", lambda eng: eng.bn_stats(smt[:, 40:46], x2b[:, 0:512]))
            p.op("v", lambda eng: eng.bn_stats(smt[:, 46:52], x2b[:, 512:1024]))
            p.op("v", lambda eng: eng.drain())
            p.op("v", lambda eng: eng.bn_aggr(smt[:, 52:54], smt[:, 40:52]))
            tv2 = p.dinc("v")
            p.wait("s", tv2)
            p.op("s", lambda eng: eng.activation(out=smt[:, 54:55], in_=smt[:, 53:54],
                                                 func=AF.Sqrt, bias=cst[:, 0:1],
                                                 scale=1.0))
            ts2 = p.dinc("s")
            p.wait("v", ts2)
            p.op("v", lambda eng: eng.reciprocal(smt[:, 54:55], smt[:, 54:55]))
            p.op("v", lambda eng: eng.drain())
            p.op("v", lambda eng: eng.tensor_scalar(out=ktm(), in0=x2b[:, :],
                                                    scalar1=smt[:, 52:53],
                                                    scalar2=smt[:, 54:55],
                                                    op0=ALU.subtract, op1=ALU.mult))
            p.op("v", lambda eng: eng.bn_stats(smt[:, 40:46], ktm(0, 512)))
            p.op("v", lambda eng: eng.bn_stats(smt[:, 46:52], ktm(512, 1024)))
            p.op("v", lambda eng: eng.drain())
            p.op("v", lambda eng: eng.bn_aggr(smt[:, 52:54], smt[:, 40:52]))
            tv3 = p.dinc("v")
            p.wait("s", tv3)
            p.op("s", lambda eng: eng.activation(out=smt[:, 54:55], in_=smt[:, 53:54],
                                                 func=AF.Sqrt, bias=cst[:, 0:1],
                                                 scale=1.0))
            ts3 = p.dinc("s")
            p.wait("v", ts3)
            p.op("v", lambda eng: eng.reciprocal(smt[:, 54:55], smt[:, 54:55]))
            p.op("v", lambda eng: eng.drain())
            p.op("v", lambda eng: eng.tensor_scalar(out=qtm(), in0=ktm(),
                                                    scalar1=smt[:, 52:53],
                                                    scalar2=smt[:, 54:55],
                                                    op0=ALU.subtract, op1=ALU.mult))
            p.op("v", lambda eng: eng.tensor_tensor(out=qtm(), in0=qtm(),
                                                    in1=gbg(), op=ALU.mult))
            p.op("v", lambda eng: eng.tensor_tensor(out=qtm(), in0=qtm(),
                                                    in1=gbb(), op=ALU.add))
            tv4 = p.dinc("v")
            rSTAT.w = tv4
            tap("xn3_%d" % i, qtm(), tv4, (128, 1024), rQ)

            if _STAGE < 13:
                return
            p.wait("s", tv4)
            p.op("s", lambda eng: eng.activation(out=vtm(), in_=qtm(),
                                                 func=AF.Square,
                                                 accum_out=smt[:, 32:33]))
            p.op("s", lambda eng: eng.drain())
            p.op("s", lambda eng: eng.activation(out=smt[:, 33:34], in_=smt[:, 32:33],
                                                 func=AF.Sqrt))
            ts5 = p.dinc("s")
            p.wait("v", ts5)
            p.claim("v", rX2)
            p.op("v", lambda eng: eng.tensor_scalar(out=smt[:, 33:34], in0=smt[:, 33:34],
                                                    scalar1=NORM_EPS, scalar2=None,
                                                    op0=ALU.max))
            p.op("v", lambda eng: eng.reciprocal(smt[:, 34:35], smt[:, 33:34]))
            p.op("v", lambda eng: eng.drain())
            p.op("v", lambda eng: eng.tensor_tensor(out=smt[:, 34:35], in0=smt[:, 34:35],
                                                    in1=scl[:, :], op=ALU.mult))
            p.op("v", lambda eng: eng.tensor_scalar(out=khtm(), in0=qtm(),
                                                    scalar1=smt[:, 34:35], scalar2=None,
                                                    op0=ALU.mult))
            tv5 = p.dinc("v")
            tap("xq_%d" % i, khtm(), tv5, (128, 1024), rKH)

            if _STAGE < 14:
                return
            p.wait("t", tv5)
            p.claim("t", rPS[(1, 0)], rPS[(1, 1)])
            for j in range(8):
                half, qtr = (0, j) if j < 4 else (1, j - 4)
                p.op("t", lambda eng, j=j, half=half, qtr=qtr: eng.transpose(
                    PA(1, half, qtr * 128, (qtr + 1) * 128),
                    khtm(j * 128, (j + 1) * 128), idn[:, :]))
            tt2 = p.dinc("t")
            rPS[(1, 0)].w = tt2; rPS[(1, 1)].w = tt2
            p.wait("s", tt2)
            p.claim("s", rXNT)
            p.op("s", lambda eng: eng.copy(xnT(0, 512), PA(1, 0)))
            p.op("s", lambda eng: eng.copy(xnT(512, 1024), PA(1, 1)))
            ts6 = p.dinc("s")
            rXNT.w = ts6
            rPS[(1, 0)].r.append(ts6); rPS[(1, 1)].r.append(ts6)
            rX2.r.append(ts6)

            if _STAGE < 15:
                return
            p.use("t", rXNT, rMU)
            p.claim("t", rPS[(0, 0)], rPS[(0, 1)])
            for k in range(8):
                p.op("t", lambda eng, k=k: eng.matmul(
                    PA(0, 0, 0, 128), mu[:, k * 256: k * 256 + 128],
                    xnT(k * 128, (k + 1) * 128), start=(k == 0), stop=(k == 7)))
                p.op("t", lambda eng, k=k: eng.matmul(
                    PA(0, 1, 0, 128), mu[:, k * 256 + 128: (k + 1) * 256],
                    xnT(k * 128, (k + 1) * 128), start=(k == 0), stop=(k == 7)))
            tlg = p.dinc("t")
            rPS[(0, 0)].w = tlg; rPS[(0, 1)].w = tlg
            rMU.r.append(tlg)
            p.wait("s", tlg)
            p.claim("s", rLG)
            p.op("s", lambda eng: eng.copy(lgsb[:, 0:128], PA(0, 0, 0, 128)))
            p.op("s", lambda eng: eng.copy(lgsb[:, 128:256], PA(0, 1, 0, 128)))
            tlge = p.dinc("s")
            rLG.w = tlge
            rPS[(0, 0)].r.append(tlge); rPS[(0, 1)].r.append(tlge)
            tap("lg_%d" % i, lgsb[:, :], tlge, (128, 256), rLG)
            p.wait("g", tlge)
            rLG.r.append(p.dma("g", lg_hbm[0, :, i * C:(i + 1) * C], lgsb[:, 0:128]))
            rLG.r.append(p.dma("g", lg_hbm[1, :, i * C:(i + 1) * C], lgsb[:, 128:256]))

            if _STAGE < 16:
                return
            for j in range(2):
                p.use("v", rLG, rFMZ)
                p.op("v", lambda eng, j=j: eng.reduce_max(
                    smt[:, 35:36], lgsb[:, j * 128:(j + 1) * 128], axis=AXL.X))
                p.op("v", lambda eng: eng.drain())
                p.op("v", lambda eng, j=j: eng.tensor_tensor(
                    out=smt[:, 36:37], in0=fm[:, j:j + 1], in1=smt[:, 35:36], op=ALU.max))
                p.op("v", lambda eng, j=j: eng.tensor_tensor(
                    out=smt[:, 37:38], in0=fm[:, j:j + 1], in1=smt[:, 36:37],
                    op=ALU.subtract))
                p.op("v", lambda eng, j=j: eng.tensor_copy(fm[:, j:j + 1],
                                                           smt[:, 36:37]))
                p.op("v", lambda eng: eng.tensor_scalar(
                    out=smt[:, 38:39], in0=smt[:, 36:37], scalar1=-1.0, scalar2=None,
                    op0=ALU.mult))
                tf1 = p.dinc("v")
                tap("fm_%d_%d" % (i, j), fm[:, :], tf1, (128, 2), rFMZ)
                p.wait("s", tf1)
                p.claim("s", rPSB)
                p.op("s", lambda eng: eng.activation(out=smt[:, 39:40],
                                                     in_=smt[:, 37:38], func=AF.Exp))
                p.op("s", lambda eng, j=j: eng.activation(
                    out=psb[:, 0:128],
                    in_=lgsb[:, j * 128:(j + 1) * 128],
                    func=AF.Exp, bias=smt[:, 38:39], scale=1.0,
                    accum_out=smt[:, 35:36]))
                tf2 = p.dinc("s")
                rPSB.w = tf2
                p.wait("v", tf2)
                p.op("v", lambda eng, j=j: eng.scalar_tensor_tensor(
                    out=fz[:, j:j + 1], in0=fz[:, j:j + 1], scalar=smt[:, 39:40],
                    in1=smt[:, 35:36], op0=ALU.mult, op1=ALU.add))
                tf3 = p.dinc("v")
                rFMZ.w = tf3
                tap("fz_%d_%d" % (i, j), fz[:, :], tf3, (128, 2), rFMZ)
                bk = (2, j)
                p.wait("t", tf2)
                p.claim("t", rPS[bk])
                p.op("t", lambda eng, j=j, bk=bk: eng.transpose(
                    PA(bk[0], bk[1], 0, 128), psb[:, 0:128],
                    idn[:, :]))
                tf4 = p.dinc("t")
                rPS[bk].w = tf4
                p.wait("s", tf4)
                if j == 0:
                    p.claim("s", rLG)
                p.op("s", lambda eng, j=j, bk=bk: eng.copy(
                    lgsb[:, j * 128:(j + 1) * 128], PA(bk[0], bk[1], 0, 128)))
                tf5 = p.dinc("s")
                tap("pt_%d_%d" % (i, j), lgsb[:, j * 128:(j + 1) * 128], tf5,
                    (128, 128), rLG)
                rPTM.w = tf5
                rPS[bk].r.append(tf5)
                rPSB.r.append(tf5)
                pd = 3 if j == 0 else 1
                p.wait("t", tf5)
                p.claim("t", rPS[(pd, 0)], rPS[(pd, 1)])
                for half in range(2):
                    p.op("t", lambda eng, j=j, pd=pd, half=half: eng.matmul(
                        PA(pd, half), lgsb[:, j * 128:(j + 1) * 128],
                        qtm(half * 512, (half + 1) * 512), start=True, stop=True))
                tf6 = p.dinc("t")
                rPS[(pd, 0)].w = tf6; rPS[(pd, 1)].w = tf6
                p.wait("v", tf6); p.use("v", rAACC)
                p.op("v", lambda eng, j=j, pd=pd: eng.scalar_tensor_tensor(
                    out=aacc[:, j * 1024:(j + 1) * 1024],
                    in0=aacc[:, j * 1024:(j + 1) * 1024],
                    scalar=smt[:, 39:40], in1=PT[pd][:, :], op0=ALU.mult, op1=ALU.add))
                tf7 = p.dinc("v")
                rAACC.w = tf7
                tap("aac_%d_%d" % (i, j), aacc[:, j * 1024:(j + 1) * 1024], tf7,
                    (128, 1024), rAACC)
                rPS[(pd, 0)].r.append(tf7); rPS[(pd, 1)].r.append(tf7)
                rLG.r.append(tf6)

            endtok = rAACC.w
            for r_ in (rQ, rK, rKH, rV, rQHT, rKHT):
                r_.r.append(endtok)

        for i in range(NCH if _NCHA is None else _NCHA):
            chunk_A(i)

        endA = [("sT", p.cnt["sT"]), ("sV", p.cnt["sV"]), ("sS", p.cnt["sS"])]
        for r_ in (rW1B[0], rW1B[1], rW2B[0], rW2B[1], rSOT, rMU):
            r_.r = r_.r + list(endA)
        p.claim("g", rMU)
        tb2r = p.dma("g", mu[:, 0:2048], b2r_h)
        rMU.w = tb2r

        if not _DO_B:
            p.wait("y", ("dG", p.cnt["dG"]))
            p.wait("y", ("dY", p.cnt["dY"]))
            return _realize(nc, stk, p)
        p.use("v", rAACC, rFMZ)
        p.claim("v", rAACC, rFMZ)
        p.op("v", lambda eng: eng.reciprocal(smt[:, 32:34], fz[:, :]))
        p.op("v", lambda eng: eng.drain())
        for j in range(2):
            p.op("v", lambda eng, j=j: eng.tensor_scalar(
                out=aacc[:, j * 1024:(j + 1) * 1024],
                in0=aacc[:, j * 1024:(j + 1) * 1024],
                scalar1=sm1[:, j:j + 1], scalar2=None, op0=ALU.mult))
        tB1 = p.dinc("v")
        rAACC.w = tB1
        tap("slotin", aacc[:, :], tB1, (128, 2048), rAACC)

        p.wait("t", tB1)
        p.claim("t", *[rPS[(b_, hf)] for b_ in range(4) for hf in range(2)])
        for q in range(16):
            k, j = q // 2, q % 2
            b_, hf, qt = q // 4, (q % 4) // 2, q % 2
            p.op("t", lambda eng, k=k, j=j, b_=b_, hf=hf, qt=qt: eng.transpose(
                PA(b_, hf, qt * 128, (qt + 1) * 128),
                aacc[:, j * 1024 + k * 128: j * 1024 + (k + 1) * 128], idn[:, :]))
        tB2 = p.dinc("t")
        for b_ in range(4):
            rPS[(b_, 0)].w = tB2; rPS[(b_, 1)].w = tB2
        p.wait("s", tB2)
        p.claim("s", rSIT)
        for b_ in range(4):
            for hf in range(2):
                p.op("s", lambda eng, b_=b_, hf=hf: eng.copy(
                    siT(b_ * 512 + hf * 256, b_ * 512 + hf * 256 + 256),
                    PA(b_, hf, 0, 256)))
        tB3 = p.dinc("s")
        rSIT.w = tB3
        for b_ in range(4):
            rPS[(b_, 0)].r.append(tB3); rPS[(b_, 1)].r.append(tB3)
        rAACC.r.append(tB3)
        tap("siT", siT(), tB3, (128, 2048), rSIT)

        for e in range(E):
            be = e % 2
            p.claim("g", rW1B[be], rW2B[be])
            tw1 = p.dma("g", w1b[be]().rearrange("p (k h) -> p k h", k=8),
                        w1_h[e, :, :].rearrange("(k p) h -> p k h", p=128))
            tw2 = p.dma("g", w2b[be]().rearrange("p (k d) -> p k d", k=2),
                        w2_h[e, :, :].rearrange("(k p) d -> p k d", p=128))
            rW1B[be].w = tw1; rW2B[be].w = tw2

            pb = e % 2
            p.waits("t", [tw1, tB3]); p.use("t", rSIT)
            p.claim("t", rPS[(pb, 0)], rPS[(pb, 1)])
            for m in range(2):
                for k in range(8):
                    p.op("t", lambda eng, e=e, m=m, k=k, pb=pb: eng.matmul(
                        PA(pb, m, 0, 16),
                        w1b[e % 2](k * 256 + m * 128, k * 256 + (m + 1) * 128),
                        siT(k * 256 + e * 16, k * 256 + (e + 1) * 16),
                        start=(k == 0), stop=(k == 7)))
            tBh = p.dinc("t")
            rPS[(pb, 0)].w = tBh; rPS[(pb, 1)].w = tBh
            rW1B[be].r.append(tBh); rSIT.r.append(tBh)
            p.wait("s", tBh)
            p.claim("s", rHTS)
            for m in range(2):
                p.op("s", lambda eng, e=e, m=m, pb=pb: eng.activation(
                    out=smt[:, 64 + m * 16:64 + (m + 1) * 16], in_=PA(pb, m, 0, 16),
                    func=AF.Gelu, bias=b1c[:, m * 16 + e: m * 16 + e + 1],
                    scale=1.0))
            tBg = p.dinc("s")
            rHTS.w = tBg
            rPS[(pb, 0)].r.append(tBg); rPS[(pb, 1)].r.append(tBg)

            po = 2 + e % 2
            p.waits("t", [tBg, tw2])
            p.claim("t", rPS[(po, 0)], rPS[(po, 1)])
            for m in range(2):
                for half in range(2):
                    p.op("t", lambda eng, e=e, m=m, half=half, po=po: eng.matmul(
                        PA(po, half, 0, 512, 0, 16), smt[:, 64 + m * 16:64 + (m + 1) * 16],
                        w2b[e % 2](m * 1024 + half * 512, m * 1024 + (half + 1) * 512),
                        start=(m == 0), stop=(m == 1)))
            tBs = p.dinc("t")
            rPS[(po, 0)].w = tBs; rPS[(po, 1)].w = tBs
            rW2B[be].r.append(tBs); rHTS.r.append(tBs)
            p.wait("s", tBs)
            p.claim("s", rSOT)
            p.op("s", lambda eng, po=po: eng.copy(x2b[0:16, 0:1024],
                                                  PT[po][0:16, :]))
            tBt = p.dinc("s")
            rSOT.w = tBt
            rPS[(po, 0)].r.append(tBt); rPS[(po, 1)].r.append(tBt)
            p.wait("g", tBt)
            dst = aacc[(e % 8) * 16:(e % 8) * 16 + 16,
                       (e // 8) * 1024:(e // 8) * 1024 + 1024]
            rSOT.r.append(p.dma("g", dst, x2b[0:16, 0:1024]))
        tok_so = ("dG", p.cnt["dG"])
        p.wait("v", tok_so)
        p.use("v", rMU)
        for j in range(2):
            p.op("v", lambda eng, j=j: eng.tensor_tensor(
                out=so[j](), in0=so[j](), in1=mu[:, j * 1024:(j + 1) * 1024],
                op=ALU.add))
        tok_so_v = p.dinc("v")
        rAACC.w = tok_so_v
        tap("slotout", aacc[:, :], tok_so_v, (128, 2048), rAACC)

        if not _DO_C:
            p.wait("y", ("dG", p.cnt["dG"]))
            p.wait("y", ("dY", p.cnt["dY"]))
            return _realize(nc, stk, p)
        def chunk_C(i):
            sl = i % 2
            p.wait("y", tok_so)
            p.claim("y", rX2C[sl], rW2B[0], rW2B[1])
            rX2C[sl].w = p.dma("y", x2c[sl](), x2_hbm[i * C:(i + 1) * C, :])
            p.claim("y", rLGC[sl])
            p.dma("y", lgc[sl](0, 128), lg_hbm[0, :, i * C:(i + 1) * C])
            rLGC[sl].w = p.dma("y", lgc[sl](128, 256), lg_hbm[1, :, i * C:(i + 1) * C])

            b_ = sl
            p.use("t", rLGC[sl])
            p.claim("t", rPS[(b_, 0)])
            p.op("t", lambda eng, sl=sl, b_=b_: eng.transpose(
                PA(b_, 0, 0, 128), lgc[sl](0, 128), idn[:, :]))
            p.op("t", lambda eng, sl=sl, b_=b_: eng.transpose(
                PA(b_, 0, 128, 256), lgc[sl](128, 256), idn[:, :]))
            tc1 = p.dinc("t")
            rPS[(b_, 0)].w = tc1
            p.wait("s", tc1)
            p.claim("s", rLTM, rSIT)
            p.op("s", lambda eng, b_=b_: eng.copy(ltm(0, 256), PA(b_, 0, 0, 256)))
            tc2 = p.dinc("s")
            rLTM.w = tc2
            rPS[(b_, 0)].r.append(tc2)
            rLGC[sl].r.append(tc2)
            p.use("v", rLTM)
            p.op("v", lambda eng: eng.reduce_max(smt[:, 32:33], ltm(0, 256), axis=AXL.X))
            p.op("v", lambda eng: eng.drain())
            p.op("v", lambda eng: eng.tensor_scalar(out=smt[:, 33:34], in0=smt[:, 32:33],
                                                    scalar1=-1.0, scalar2=None,
                                                    op0=ALU.mult))
            tc3 = p.dinc("v")
            p.wait("s", tc3)
            p.claim("s", rPTC)
            p.op("s", lambda eng: eng.activation(out=ptc(0, 256), in_=ltm(0, 256),
                                                 func=AF.Exp, bias=smt[:, 33:34],
                                                 scale=1.0, accum_out=smt[:, 34:35]))
            tc4 = p.dinc("s")
            rPTC.w = tc4
            rLTM.r.append(tc4)
            p.wait("v", tc4)
            p.op("v", lambda eng: eng.reciprocal(smt[:, 35:36], smt[:, 34:35]))
            p.op("v", lambda eng: eng.drain())
            p.op("v", lambda eng: eng.tensor_scalar(out=ptc(0, 256), in0=ptc(0, 256),
                                                    scalar1=smt[:, 35:36], scalar2=None,
                                                    op0=ALU.mult))
            tc5 = p.dinc("v")
            rPTC.w = tc5
            p.wait("t", tc5)
            p.claim("t", rPS[(b_, 1)])
            p.op("t", lambda eng, b_=b_: eng.transpose(PA(b_, 1, 0, 128),
                                                       ptc(0, 128), idn[:, :]))
            p.op("t", lambda eng, b_=b_: eng.transpose(PA(b_, 1, 128, 256),
                                                       ptc(128, 256), idn[:, :]))
            tc6 = p.dinc("t")
            rPS[(b_, 1)].w = tc6
            rPTC.r.append(tc6)
            p.wait("s", tc6)
            p.claim("s", rCT)
            p.op("s", lambda eng, b_=b_: eng.copy(ct(0, 256), PA(b_, 1, 0, 256)))
            tc7 = p.dinc("s")
            rCT.w = tc7
            rPS[(b_, 1)].r.append(tc7)
            po = 2 + sl
            p.use("t", rCT, rAACC)
            p.wait("t", tok_so)
            p.claim("t", rPS[(po, 0)], rPS[(po, 1)])
            for j in range(2):
                for half in range(2):
                    p.op("t", lambda eng, j=j, half=half, po=po: eng.matmul(
                        PA(po, half), ct(j * 128, (j + 1) * 128),
                        so[j](half * 512, (half + 1) * 512),
                        start=(j == 0), stop=(j == 1)))
            tc8 = p.dinc("t")
            rPS[(po, 0)].w = tc8; rPS[(po, 1)].w = tc8
            rCT.r.append(tc8)
            p.wait("v", tc8); p.use("v", rX2C[sl])
            p.claim("v", rOUTC[sl])
            p.op("v", lambda eng, sl=sl, po=po: eng.tensor_tensor(
                out=outc[sl](), in0=x2c[sl](), in1=PT[po][:, :], op=ALU.add))
            tc9 = p.dinc("v")
            rOUTC[sl].w = tc9
            rX2C[sl].r.append(tc9)
            rPS[(po, 0)].r.append(tc9); rPS[(po, 1)].r.append(tc9)
            p.wait("g", tc9)
            rOUTC[sl].r.append(p.dma("g", y_h[i * C:(i + 1) * C, :], outc[sl]()))

        for i in range(NCH):
            chunk_C(i)

        p.wait("y", ("dG", p.cnt["dG"]))
        p.wait("y", ("dY", p.cnt["dY"]))
        return _realize(nc, stk, p)


def _realize(nc, stk, p):
    block = stk.enter_context(nc.Block())

    @block.tensor
    def _(eng):
        for fn in p.stream["t"]:
            fn(eng)

    @block.vector
    def _(eng):
        for fn in p.stream["v"]:
            fn(eng)

    @block.scalar
    def _(eng):
        for fn in p.stream["s"]:
            fn(eng)

    @block.gpsimd
    def _(eng):
        for fn in p.stream["g"]:
            fn(eng)

    @block.sync
    def _(eng):
        for fn in p.stream["y"]:
            fn(eng)
    return nc


def _host_consts():
    import jax
    import jax.numpy as jnp
    half = DH // 2
    with jax.default_device(jax.devices("cpu")[0]):
        inv = 1.0 / ROPE_BASE ** (jnp.arange(half, dtype=jnp.float32) / half)
        freqs = jnp.arange(L, dtype=jnp.float32)[:, None] * inv[None, :]
        cos = np.asarray(jnp.cos(freqs), np.float32)
        sin = np.asarray(jnp.sin(freqs), np.float32)
    costab = np.empty((L, DH), np.float32)
    sintab = np.empty((L, DH), np.float32)
    costab[:, 0::2] = cos; costab[:, 1::2] = cos
    sintab[:, 0::2] = -sin; sintab[:, 1::2] = sin
    s_idx = np.arange(128)[:, None]
    t_idx = np.arange(128)[None, :]
    cstv = np.zeros((128, 4), np.float32)
    cstv[:, 0] = LN_EPS
    return {
        "costab": costab,
        "sintab": sintab,
        "identity": np.eye(128, dtype=np.float32),
        "maskut": (s_idx <= t_idx).astype(np.float32),
        "onescol": np.ones((128, 1), np.float32),
        "ones8": np.ones((H, 128), np.float32),
        "cst": cstv,
    }


def _host_inputs(L0_attn_x, L0_attn_Wq, L0_attn_Wk, L0_attn_Wv, L0_attn_Wbeta,
                 L0_attn_bbeta, L0_attn_Walpha, L0_attn_balpha, L0_attn_Wo,
                 L0_attn_bo, L0_ffn_ln_gamma, L0_ffn_ln_beta, L0_ffn_mu,
                 L0_ffn_scale, L0_ffn_w1, L0_ffn_b1, L0_ffn_w2, L0_ffn_b2):
    f = np.float32
    x = np.ascontiguousarray(np.asarray(L0_attn_x, f))
    wab = np.concatenate([np.asarray(L0_attn_Wbeta, f),
                          np.asarray(L0_attn_Walpha, f)], 0)
    b1 = np.asarray(L0_ffn_b1, f)
    b1cols = np.zeros((128, 32), f)
    for m in range(2):
        for e in range(E):
            b1cols[:, m * 16 + e] = b1[e, m * 128:(m + 1) * 128]
    shared = {
        "wqT": np.ascontiguousarray(np.asarray(L0_attn_Wq, f).T),
        "wkT": np.ascontiguousarray(np.asarray(L0_attn_Wk, f).T),
        "wvT": np.ascontiguousarray(np.asarray(L0_attn_Wv, f).T),
        "woT": np.ascontiguousarray(np.asarray(L0_attn_Wo, f).T),
        "wabT": np.ascontiguousarray(wab.T),
        "boflat": np.ascontiguousarray(np.asarray(L0_attn_bo, f).reshape(1, D)),
        "muflat": np.ascontiguousarray(np.asarray(L0_ffn_mu, f).reshape(D, ES)),
        "gbg": np.ascontiguousarray(
            np.broadcast_to(np.asarray(L0_ffn_ln_gamma, f), (128, D))),
        "gbb": np.ascontiguousarray(
            np.broadcast_to(np.asarray(L0_ffn_ln_beta, f), (128, D))),
        "sclbc": np.ascontiguousarray(np.broadcast_to(
            np.asarray(L0_ffn_scale, f).reshape(1, 1), (128, 1))),
        "b1cols": b1cols,
        "b2rep": np.ascontiguousarray(np.concatenate(
            [np.repeat(np.asarray(L0_ffn_b2, f), NS, axis=0)[0:128],
             np.repeat(np.asarray(L0_ffn_b2, f), NS, axis=0)[128:256]], axis=1)),
        "w1": np.ascontiguousarray(np.asarray(L0_ffn_w1, f)),
        "w2": np.ascontiguousarray(np.asarray(L0_ffn_w2, f)),
    }
    shared.update(_host_consts())
    cstv = shared["cst"].copy()
    cstv[0:8, 1] = np.asarray(L0_attn_bbeta, f)
    cstv[0:8, 2] = np.asarray(L0_attn_balpha, f)
    shared["cst"] = cstv
    in_maps = []
    for b in range(B):
        m = dict(shared)
        m["x"] = np.ascontiguousarray(x[b])
        in_maps.append(m)
    return in_maps


_NC = None


def kernel(**inputs):
    global _NC
    if _NC is None:
        _NC = _build()
    in_maps = _host_inputs(**inputs)
    res = run_bass_kernel_spmd(_NC, in_maps, core_ids=list(range(B)))
    out = np.stack([np.asarray(res.results[b]["y"]) for b in range(B)], axis=0)
    return np.ascontiguousarray(out.astype(np.float32))


# version 05
# speedup vs baseline: 1.0245x; 1.0245x over previous
import numpy as np
from contextlib import ExitStack

import concourse.bass as bass
import concourse.mybir as mybir
from concourse.bass_utils import run_bass_kernel_spmd

F32 = mybir.dt.float32
AF = mybir.ActivationFunctionType
ALU = mybir.AluOpType
AXL = mybir.AxisListType

B, L, D = 8, 4096, 1024
H, DH = 8, 128
C = 128
NCH = L // C
E, NS, HID, ES = 16, 16, 256, 256
LN_EPS = 1e-5
NORM_EPS = 1e-12
ROPE_BASE = 10000.0

_STAGE = 99
_S6 = 9
_DO_B = True
_DO_C = True
_NCHA = None

_BSPAN = 0.999 - 0.85
_D_HI = float(np.float32(_BSPAN)); _D_LO = float(np.float32(_BSPAN - _D_HI))
_B_HI = float(np.float32(0.85));   _B_LO = float(np.float32(0.85 - _B_HI))


class _Res:
    __slots__ = ("w", "r")

    def __init__(self):
        self.w = None
        self.r = []


class _Seg:
    def __init__(self, h, off, width):
        self.h, self.off, self.wd = h, off, width

    def __call__(self, a=0, b=None):
        b = self.wd if b is None else b
        return self.h[:, self.off + a:self.off + b]


class _P:
    ESEM = {"t": "sT", "v": "sV", "s": "sS"}
    QSEM = {"y": "dY", "g": "dG"}

    def __init__(self):
        self.stream = {e: [] for e in "tvsgy"}
        self.cnt = {n: 0 for n in ("sT", "sV", "sS", "dY", "dG")}
        self.seen = {}
        self.sems = None

    def wait(self, e, tok):
        if tok is None:
            return
        s, v = tok
        if self.seen.get((e, s), 0) >= v:
            return
        self.seen[(e, s)] = v
        self.stream[e].append(lambda eng, s=s, v=v: eng.wait_ge(self.sems[s], v))

    def waits(self, e, toks):
        for t in toks:
            self.wait(e, t)

    def use(self, e, *ress):
        for r in ress:
            self.wait(e, r.w)

    def claim(self, e, *ress):
        for r in ress:
            for t in r.r:
                self.wait(e, t)
            r.r = []

    def op(self, e, fn):
        self.stream[e].append(fn)

    def dinc(self, e):
        s = self.ESEM[e]
        self.cnt[s] += 1
        v = self.cnt[s]
        self.stream[e].append(lambda eng, s=s: eng.drain().then_inc(self.sems[s], 1))
        return (s, v)

    def dma(self, e, out, in_):
        s = self.QSEM[e]
        self.cnt[s] += 16
        v = self.cnt[s]
        self.stream[e].append(
            lambda eng, o=out, i=in_: eng.dma_start(out=o, in_=i).then_inc(self.sems[s], 16))
        return (s, v)


def _build(taps=()):
    taps = set(taps)
    nc = bass.Bass("TRN2", target_bir_lowering=False, detect_race_conditions=False)

    dt = nc.dram_tensor
    x_h = dt("x", [L, D], F32, kind="ExternalInput").ap()
    wq_h = dt("wqT", [D, D], F32, kind="ExternalInput").ap()
    wk_h = dt("wkT", [D, D], F32, kind="ExternalInput").ap()
    wv_h = dt("wvT", [D, D], F32, kind="ExternalInput").ap()
    wo_h = dt("woT", [D, D], F32, kind="ExternalInput").ap()
    wab_h = dt("wabT", [D, 16], F32, kind="ExternalInput").ap()
    bo_h = dt("boflat", [1, D], F32, kind="ExternalInput").ap()
    mu_h = dt("muflat", [D, ES], F32, kind="ExternalInput").ap()
    gbg_h = dt("gbg", [128, D], F32, kind="ExternalInput").ap()
    gbb_h = dt("gbb", [128, D], F32, kind="ExternalInput").ap()
    scl_h = dt("sclbc", [128, 1], F32, kind="ExternalInput").ap()
    b1c_h = dt("b1cols", [128, 32], F32, kind="ExternalInput").ap()
    b2r_h = dt("b2rep", [128, 2 * D], F32, kind="ExternalInput").ap()
    w1_h = dt("w1", [E, D, HID], F32, kind="ExternalInput").ap()
    w2_h = dt("w2", [E, HID, D], F32, kind="ExternalInput").ap()
    cos_h = dt("costab", [L, DH], F32, kind="ExternalInput").ap()
    sin_h = dt("sintab", [L, DH], F32, kind="ExternalInput").ap()
    idn_h = dt("identity", [128, 128], F32, kind="ExternalInput").ap()
    msk_h = dt("maskut", [128, 128], F32, kind="ExternalInput").ap()
    oc_h = dt("onescol", [128, 1], F32, kind="ExternalInput").ap()
    o8_h = dt("ones8", [H, 128], F32, kind="ExternalInput").ap()
    cst_h = dt("cst", [128, 4], F32, kind="ExternalInput").ap()

    x2_hbm = dt("x2scr", [L, D], F32).ap()
    lg_hbm = dt("lgscr", [2, 128, L], F32).ap()
    y_h = dt("y", [L, D], F32, kind="ExternalOutput").ap()

    with ExitStack() as stk:
        def sb(name, shape):
            return stk.enter_context(nc.sbuf_tensor(name, shape, F32))

        wq = sb("wq", [128, 8 * D]); wk = sb("wk", [128, 8 * D])
        wv = sb("wv", [128, 8 * D]); wo = sb("wo", [128, 8 * D])
        wab = sb("wab", [128, 8 * 16])
        mu = sb("mu", [128, 8 * ES])
        idn = sb("idn", [128, 128]); msk = sb("msk", [128, 128])
        ocol = sb("ocol", [128, 1])
        o8 = sb("o8s", [H, 128])
        orow = o8[0:1, 0:128]
        b1c = sb("b1cs", [128, 32])
        scl = sb("scls", [128, 1]); cst = sb("csts", [128, 4])

        poolA = sb("poolA", [128, 4096])
        qtm = _Seg(poolA, 0, 1024)
        ktm = _Seg(poolA, 1024, 1024)
        vtm = _Seg(poolA, 2048, 1024)
        gbg = _Seg(poolA, 3072, 1024)
        w1b = [_Seg(poolA, 0, 2048), _Seg(poolA, 2048, 2048)]

        poolB = sb("poolB", [128, 4096])
        qhT = _Seg(poolB, 0, 1024)
        khT = _Seg(poolB, 1024, 1024)
        xn1 = _Seg(poolB, 2048, 1024)
        gbb = _Seg(poolB, 3072, 1024)
        w2b = [_Seg(poolB, 0, 2048), _Seg(poolB, 2048, 2048)]
        x2c = [_Seg(poolB, 0, 1024), _Seg(poolB, 1024, 1024)]
        outc = [_Seg(poolB, 2048, 1024), _Seg(poolB, 3072, 1024)]

        poolC = sb("poolC", [128, 2048])
        xnT = _Seg(poolC, 0, 1024)
        siT = _Seg(poolC, 0, 2048)
        ltm = _Seg(poolC, 0, 256)
        ptc = _Seg(poolC, 256, 256)
        ct = _Seg(poolC, 512, 256)
        lgc = [_Seg(poolC, 1024, 256), _Seg(poolC, 1280, 256)]

        xb = [sb("xb0", [128, D]), sb("xb1", [128, D])]
        csb = sb("csb", [128, 256])
        x2b = sb("x2b", [128, D])
        swp = _Seg(x2b, 0, 1024)
        khtm = swp
        am = [sb("am0", [128, 128]), sb("am1", [128, 128])]
        lgsb = sb("lgsb", [128, 256])
        psb = sb("psb", [128, 128])
        aacc = sb("aacc", [128, 2 * D])
        so = [_Seg(aacc, 0, 1024), _Seg(aacc, 1024, 1024)]
        s8 = sb("s8", [128, 8 * 128])
        fm = sb("fm", [128, 2]); fz = sb("fz", [128, 2])
        sg16 = sb("sg16", [8, 256])
        smt = sb("smt", [128, 96])
        witm = smt[:, 0:8]; dtm = smt[:, 8:16]; wstm = smt[:, 16:24]
        dbc = smt[:, 24:32]; sm1 = smt[:, 32:40]; st6 = smt[:, 40:52]
        mv = smt[:, 52:54]; rstd = smt[:, 54:55]; dlT = smt[0:1, 56:64]
        hts = smt[:, 64:96]

        PT = [stk.enter_context(nc.psum_tensor("p%d" % i, [128, 1024], F32))
              for i in range(4)]

        bb1 = am[0][0:8, 0:128]
        bb2 = am[1][0:8, 0:128]
        dhm = psb[0:8, 0:128]
        wih = am[0][0:8, 0:128]
        wsh = am[1][0:8, 0:128]
        mun = lgsb[0:1, 0:256]
        bo = poolC[0:1, 1024:2048]

        sems = {n: stk.enter_context(nc.semaphore(n))
                for n in ("sT", "sV", "sS", "dY", "dG")}

        p = _P()
        p.sems = sems

        rPS = {(i, hf): _Res() for i in range(4) for hf in range(2)}
        rXB = [_Res(), _Res()]; rCS = _Res(); rX2 = _Res()
        rXN1 = _Res(); rXNT = _Res(); rQ = _Res(); rK = _Res(); rKH = _Res()
        rV = _Res(); rQHT = _Res(); rKHT = _Res(); rAM = [_Res(), _Res()]
        rSG = _Res(); rAB = _Res(); rWI = _Res(); rDBC = _Res()
        rS8 = _Res(); rAACC = _Res(); rFMZ = _Res(); rLG = _Res(); rPSB = _Res()
        rPTM = _Res(); rMU = _Res(); rSTAT = _Res()
        rW1B = [_Res(), _Res()]; rW2B = [_Res(), _Res()]; rHTS = _Res()
        rSOT = _Res(); rSIT = _Res()
        rLGC = [_Res(), _Res()]; rX2C = [_Res(), _Res()]; rOUTC = [_Res(), _Res()]
        rLTM = _Res(); rPTC = _Res(); rCT = _Res()

        def PA(i, half, a=0, b=512, pp0=0, pp1=128):
            return PT[i][pp0:pp1, half * 512 + a: half * 512 + b]

        def tap(name, ap, tok, shape, res=None):
            if name not in taps:
                return
            out = nc.dram_tensor("tap_" + name, list(shape), F32,
                                 kind="ExternalOutput").ap()
            p.wait("g", tok)
            dtok = p.dma("g", out, ap)
            if res is not None:
                res.r.append(dtok)

        for dst, src in (
            (wq[:, :].rearrange("p (k n) -> p k n", k=8),
             wq_h.rearrange("(k p) n -> p k n", p=128)),
            (wk[:, :].rearrange("p (k n) -> p k n", k=8),
             wk_h.rearrange("(k p) n -> p k n", p=128)),
            (wv[:, :].rearrange("p (k n) -> p k n", k=8),
             wv_h.rearrange("(k p) n -> p k n", p=128)),
            (wo[:, :].rearrange("p (k n) -> p k n", k=8),
             wo_h.rearrange("(k p) n -> p k n", p=128)),
            (wab[:, :].rearrange("p (k n) -> p k n", k=8),
             wab_h.rearrange("(k p) n -> p k n", p=128)),
            (mu[:, :].rearrange("p (k n) -> p k n", k=8),
             mu_h.rearrange("(k p) n -> p k n", p=128)),
            (gbg(), gbg_h), (gbb(), gbb_h),
            (idn[:, :], idn_h), (msk[:, :], msk_h),
            (ocol[:, :], oc_h),
            (o8[:, :], o8_h),
            (bo, bo_h),
            (b1c[:, :], b1c_h),
            (scl[:, :], scl_h), (cst[:, :], cst_h),
        ):
            tok_lds = p.dma("y", dst, src)

        p.wait("v", tok_lds)
        p.op("v", lambda eng: eng.memset(s8[:, :], 0.0))
        p.op("v", lambda eng: eng.memset(fm[:, :], -1e30))
        p.op("v", lambda eng: eng.memset(fz[:, :], 0.0))
        p.op("v", lambda eng: eng.memset(aacc[:, :], 0.0))
        t0 = p.dinc("v")
        rS8.w = t0; rFMZ.w = t0; rAACC.w = t0

        p.wait("s", tok_lds)
        p.op("s", lambda eng: eng.activation(out=xn1(), in_=mu[:, 0:1024],
                                             func=AF.Square))
        p.op("s", lambda eng: eng.activation(out=swp(), in_=mu[:, 1024:2048],
                                             func=AF.Square))
        tmsq = p.dinc("s")
        p.wait("t", tmsq)
        for k in range(8):
            src = xn1(k * 256, (k + 1) * 256) if k < 4 else \
                swp((k - 4) * 256, (k - 3) * 256)
            p.op("t", lambda eng, s=src, k=k: eng.matmul(
                PA(0, 0, 0, 256, 0, 1), ocol[:, :], s, start=(k == 0), stop=(k == 7)))
        tcs = p.dinc("t")
        rPS[(0, 0)].w = tcs
        rXN1.r.append(tcs)
        p.wait("s", tcs)
        p.op("s", lambda eng: eng.activation(out=mun, in_=PA(0, 0, 0, 256, 0, 1),
                                             func=AF.Sqrt))
        tsq = p.dinc("s")
        rPS[(0, 0)].r.append(tsq)
        p.wait("v", tsq)
        p.op("v", lambda eng: eng.tensor_scalar(out=mun, in0=mun,
                                                scalar1=NORM_EPS, scalar2=None,
                                                op0=ALU.max))
        p.op("v", lambda eng: eng.reciprocal(mun, mun))
        trn = p.dinc("v")
        p.wait("t", trn)
        p.claim("t", rPS[(0, 0)])
        p.op("t", lambda eng: eng.matmul(PA(0, 0, 0, 256, 0, 128), orow,
                                         mun, start=True, stop=True))
        tbc = p.dinc("t")
        rPS[(0, 0)].w = tbc
        p.wait("v", tbc)
        for k in range(8):
            p.op("v", lambda eng, k=k: eng.tensor_tensor(
                out=mu[:, k * 256:(k + 1) * 256], in0=mu[:, k * 256:(k + 1) * 256],
                in1=PA(0, 0, 0, 256), op=ALU.mult))
        tmu = p.dinc("v")
        rMU.w = tmu
        rPS[(0, 0)].r.append(tmu)

        def chunk_A(i):
            sl = i % 2
            p.claim("y", rXB[sl])
            rXB[sl].w = p.dma("y", xb[sl][:, :], x_h[i * C:(i + 1) * C, :])
            p.claim("y", rCS)
            p.dma("y", csb[:, 0:128], cos_h[i * C:(i + 1) * C, :])
            rCS.w = p.dma("y", csb[:, 128:256], sin_h[i * C:(i + 1) * C, :])

            p.use("v", rXB[sl]); p.claim("v", rSTAT)
            p.op("v", lambda eng: eng.bn_stats(smt[:, 40:46], xb[sl][:, 0:512]))
            p.op("v", lambda eng: eng.bn_stats(smt[:, 46:52], xb[sl][:, 512:1024]))
            p.op("v", lambda eng: eng.drain())
            p.op("v", lambda eng: eng.bn_aggr(smt[:, 52:54], smt[:, 40:52]))
            tv = p.dinc("v")
            p.wait("s", tv)
            p.op("s", lambda eng: eng.activation(out=smt[:, 54:55], in_=smt[:, 53:54],
                                                 func=AF.Sqrt, bias=cst[:, 0:1],
                                                 scale=1.0))
            ts_ = p.dinc("s")
            p.wait("v", ts_)
            p.op("v", lambda eng: eng.reciprocal(smt[:, 54:55], smt[:, 54:55]))
            p.op("v", lambda eng: eng.drain())
            p.claim("v", rXN1)
            p.op("v", lambda eng: eng.tensor_scalar(out=xn1(), in0=xb[sl][:, :],
                                                    scalar1=smt[:, 52:53],
                                                    scalar2=smt[:, 54:55],
                                                    op0=ALU.subtract, op1=ALU.mult))
            rXN1.w = p.dinc("v")
            rSTAT.w = rXN1.w
            tap("xb_%d" % i, xb[sl][:, :], rXN1.w, (128, 1024), rXB[sl])
            tap("mv_%d" % i, smt[:, 52:54], rXN1.w, (128, 2), rSTAT)
            tap("xn1_%d" % i, xn1(), rXN1.w, (128, 1024), rXN1)

            if _STAGE < 2:
                return
            p.use("t", rXN1)
            p.claim("t", rPS[(3, 0)], rPS[(3, 1)])
            for j in range(8):
                half, qtr = (0, j) if j < 4 else (1, j - 4)
                p.op("t", lambda eng, j=j, half=half, qtr=qtr: eng.transpose(
                    PA(3, half, qtr * 128, (qtr + 1) * 128),
                    xn1(j * 128, (j + 1) * 128), idn[:, :]))
            tt = p.dinc("t")
            rPS[(3, 0)].w = tt; rPS[(3, 1)].w = tt
            p.wait("s", tt)
            p.claim("s", rXNT)
            p.op("s", lambda eng: eng.copy(xnT(0, 512), PA(3, 0)))
            p.op("s", lambda eng: eng.copy(xnT(512, 1024), PA(3, 1)))
            rXNT.w = p.dinc("s")
            rPS[(3, 0)].r.append(rXNT.w); rPS[(3, 1)].r.append(rXNT.w)
            rXN1.r.append(rXNT.w)

            if _STAGE < 3:
                return
            p.use("t", rXNT)
            p.claim("t", rPS[(0, 0)], rPS[(0, 1)], rPS[(1, 0)], rPS[(1, 1)],
                    rPS[(2, 0)], rPS[(2, 1)], rPS[(3, 0)], rPS[(3, 1)])
            for k in range(8):
                for (pi, w_) in ((0, wq), (1, wk), (2, wv)):
                    for half in range(2):
                        p.op("t", lambda eng, pi=pi, w_=w_, half=half, k=k:
                             eng.matmul(PA(pi, half), xnT(k * 128, (k + 1) * 128),
                                        w_[:, k * 1024 + half * 512:
                                           k * 1024 + (half + 1) * 512],
                                        start=(k == 0), stop=(k == 7)))
                p.op("t", lambda eng, k=k: eng.matmul(
                    PA(3, 0, 0, 128, 0, 8), wab[:, k * 16:k * 16 + 8],
                    xnT(k * 128, (k + 1) * 128), start=(k == 0), stop=(k == 7)))
                p.op("t", lambda eng, k=k: eng.matmul(
                    PA(3, 1, 0, 128, 0, 8), wab[:, k * 16 + 8:(k + 1) * 16],
                    xnT(k * 128, (k + 1) * 128), start=(k == 0), stop=(k == 7)))
            tproj = p.dinc("t")
            for key in ((0, 0), (0, 1), (1, 0), (1, 1), (2, 0), (2, 1),
                        (3, 0), (3, 1)):
                rPS[key].w = tproj

            if _STAGE < 4:
                return
            p.wait("s", tproj)
            p.claim("s", rSG, rV)
            p.op("s", lambda eng: eng.activation(out=sg16[0:8, 0:128],
                                                 in_=PA(3, 0, 0, 128, 0, 8),
                                                 func=AF.Sigmoid, bias=cst[0:8, 1:2],
                                                 scale=1.0))
            p.op("s", lambda eng: eng.activation(out=sg16[0:8, 128:256],
                                                 in_=PA(3, 1, 0, 128, 0, 8),
                                                 func=AF.Sigmoid, bias=cst[0:8, 2:3],
                                                 scale=1.0))
            p.op("s", lambda eng: eng.copy(vtm(0, 512), PA(2, 0)))
            p.op("s", lambda eng: eng.copy(vtm(512, 1024), PA(2, 1)))
            tvv = p.dinc("s")
            rSG.w = tvv; rV.w = tvv
            rPS[(3, 0)].r.append(tvv); rPS[(3, 1)].r.append(tvv)
            rPS[(2, 0)].r.append(tvv); rPS[(2, 1)].r.append(tvv)
            tap("vtm_%d" % i, vtm(), tvv, (128, 1024), rV)

            if _STAGE < 5:
                return
            p.use("v", rSG); p.claim("v", rAB)
            p.op("v", lambda eng: eng.tensor_scalar(out=bb1, in0=sg16[0:8, 0:128],
                                                    scalar1=_D_HI, scalar2=_B_HI,
                                                    op0=ALU.mult, op1=ALU.add))
            p.op("v", lambda eng: eng.tensor_scalar(out=bb2, in0=sg16[0:8, 0:128],
                                                    scalar1=_D_LO, scalar2=_B_LO,
                                                    op0=ALU.mult, op1=ALU.add))
            p.op("v", lambda eng: eng.tensor_tensor(out=bb1, in0=bb1,
                                                    in1=bb2, op=ALU.add))
            tb = p.dinc("v")
            p.wait("s", tb)
            p.op("s", lambda eng: eng.activation(out=bb2, in_=bb1,
                                                 func=AF.Ln))
            tl = p.dinc("s")
            p.wait("v", tl)
            p.op("v", lambda eng: eng.tensor_tensor_scan(dhm, o8[:, :],
                                                         bb2, 0.0,
                                                         ALU.mult, ALU.add))
            tc = p.dinc("v")
            p.wait("s", tc)
            p.op("s", lambda eng: eng.activation(out=dhm, in_=dhm,
                                                 func=AF.Exp))
            td = p.dinc("s")
            p.wait("v", td)
            p.op("v", lambda eng: eng.reciprocal(wih, dhm))
            p.op("v", lambda eng: eng.drain())
            p.op("v", lambda eng: eng.tensor_tensor(out=wih, in0=sg16[0:8, 128:256],
                                                    in1=wih, op=ALU.mult))
            p.op("v", lambda eng: eng.tensor_scalar(out=wsh, in0=wih,
                                                    scalar1=psb[0:8, 127:128],
                                                    scalar2=None, op0=ALU.mult))
            tw = p.dinc("v")
            rAB.w = tw
            rSG.r.append(tw)
            tap("dhm_%d" % i, dhm, tw, (8, 128), rAB)
            tap("wih_%d" % i, wih, tw, (8, 128), rAM[0])

            if _STAGE < 6:
                return
            p.wait("t", tw)
            p.claim("t", rPS[(3, 1)])
            if _S6 >= 1:
                p.op("t", lambda eng: eng.transpose(PA(3, 1, 0, 8), wih,
                                                    idn[0:8, 0:8]))
            if _S6 >= 2:
                p.op("t", lambda eng: eng.transpose(PA(3, 1, 8, 16), dhm,
                                                    idn[0:8, 0:8]))
            if _S6 >= 3:
                p.op("t", lambda eng: eng.transpose(PA(3, 1, 16, 24), wsh,
                                                    idn[0:8, 0:8]))
            if _S6 >= 4:
                p.op("t", lambda eng: eng.transpose(PA(3, 1, 24, 32, 0, 1),
                                                    psb[0:8, 127:128], idn[0:8, 0:8]))
            t1 = p.dinc("t")
            rPS[(3, 1)].w = t1
            p.wait("s", t1)
            p.claim("s", rWI)
            if _S6 >= 1:
                p.op("s", lambda eng: eng.copy(witm, PA(3, 1, 0, 8)))
            if _S6 >= 2:
                p.op("s", lambda eng: eng.copy(dtm, PA(3, 1, 8, 16)))
            if _S6 >= 3:
                p.op("s", lambda eng: eng.copy(wstm, PA(3, 1, 16, 24)))
            if _S6 >= 4:
                p.op("s", lambda eng: eng.copy(dlT, PA(3, 1, 24, 32, 0, 1)))
            t2 = p.dinc("s")
            rWI.w = t2
            p.wait("t", t2)
            if _S6 >= 5:
                p.op("t", lambda eng: eng.matmul(PA(3, 1, 32, 40), orow, dlT,
                                                 start=True, stop=True))
            t3 = p.dinc("t")
            p.wait("s", t3)
            p.claim("s", rDBC)
            if _S6 >= 6:
                p.op("s", lambda eng: eng.copy(dbc, PA(3, 1, 32, 40)))
            t4 = p.dinc("s")
            rDBC.w = t4
            rPS[(3, 1)].r.append(t4)

            if _STAGE < 7:
                return
            cosb = bass.AP(csb, 0, [[256, 128], [0, 8], [1, 128]])
            sinb = bass.AP(csb, 128, [[256, 128], [0, 8], [1, 128]])
            swp_ev = bass.AP(x2b, 0, [[1024, 128], [2, 512]])
            swp_od = bass.AP(x2b, 1, [[1024, 128], [2, 512]])
            p.wait("v", tproj); p.use("v", rCS)
            p.claim("v", rQ, rX2)
            p.op("v", lambda eng: eng.tensor_copy(
                swp_od, bass.AP(PT[0], 0, [[1024, 128], [2, 512]])))
            p.op("v", lambda eng: eng.tensor_copy(
                swp_ev, bass.AP(PT[0], 1, [[1024, 128], [2, 512]])))
            p.op("v", lambda eng: eng.tensor_tensor(out=qtm(), in0=PT[0][:, :],
                                                    in1=cosb, op=ALU.mult))
            p.op("v", lambda eng: eng.tensor_tensor(out=swp(), in0=swp(), in1=sinb,
                                                    op=ALU.mult))
            p.op("v", lambda eng: eng.tensor_tensor(out=qtm(), in0=qtm(), in1=swp(),
                                                    op=ALU.add))
            p.wait("v", t2)
            for h in range(H):
                p.op("v", lambda eng, h=h: eng.tensor_scalar(
                    out=qtm(h * 128, (h + 1) * 128), in0=qtm(h * 128, (h + 1) * 128),
                    scalar1=smt[:, 8 + h:9 + h], scalar2=None, op0=ALU.mult))
            rQ.w = p.dinc("v")
            rPS[(0, 0)].r.append(rQ.w); rPS[(0, 1)].r.append(rQ.w)
            rXN1.r.append(rQ.w)
            tap("qhat_%d" % i, qtm(), rQ.w, (128, 1024), rQ)

            if _STAGE < 8:
                return
            p.claim("v", rK, rKH)
            p.op("v", lambda eng: eng.tensor_copy(
                swp_od, bass.AP(PT[1], 0, [[1024, 128], [2, 512]])))
            p.op("v", lambda eng: eng.tensor_copy(
                swp_ev, bass.AP(PT[1], 1, [[1024, 128], [2, 512]])))
            p.op("v", lambda eng: eng.tensor_tensor(out=ktm(), in0=PT[1][:, :],
                                                    in1=cosb, op=ALU.mult))
            p.op("v", lambda eng: eng.tensor_tensor(out=swp(), in0=swp(), in1=sinb,
                                                    op=ALU.mult))
            p.op("v", lambda eng: eng.tensor_tensor(out=ktm(), in0=ktm(), in1=swp(),
                                                    op=ALU.add))
            for h in range(H):
                p.op("v", lambda eng, h=h: eng.tensor_scalar(
                    out=khtm(h * 128, (h + 1) * 128), in0=ktm(h * 128, (h + 1) * 128),
                    scalar1=smt[:, 16 + h:17 + h], scalar2=None, op0=ALU.mult))
            tk = p.dinc("v")
            rK.w = tk; rKH.w = tk
            rPS[(1, 0)].r.append(tk); rPS[(1, 1)].r.append(tk)
            rCS.r.append(tk)
            tap("krot_%d" % i, ktm(), tk, (128, 1024), rK)
            tap("khat_%d" % i, khtm(), tk, (128, 1024), rKH)

            if _STAGE < 9:
                return
            p.use("t", rQ, rK)
            p.claim("t", rPS[(1, 0)], rPS[(1, 1)])
            for h in range(4):
                p.op("t", lambda eng, h=h: eng.transpose(
                    PA(1, 0, h * 128, (h + 1) * 128), qtm(h * 128, (h + 1) * 128),
                    idn[:, :]))
                p.op("t", lambda eng, h=h: eng.transpose(
                    PA(1, 1, h * 128, (h + 1) * 128), ktm(h * 128, (h + 1) * 128),
                    idn[:, :]))
            tg1 = p.dinc("t")
            rPS[(1, 0)].w = tg1; rPS[(1, 1)].w = tg1
            p.wait("s", tg1)
            p.claim("s", rQHT, rKHT)
            p.op("s", lambda eng: eng.copy(qhT(0, 512), PA(1, 0)))
            p.op("s", lambda eng: eng.copy(khT(0, 512), PA(1, 1)))
            te1 = p.dinc("s")
            rPS[(1, 0)].r.append(te1); rPS[(1, 1)].r.append(te1)
            p.claim("t", rPS[(3, 0)], rPS[(3, 1)])
            for h in range(4, 8):
                p.op("t", lambda eng, h=h: eng.transpose(
                    PA(3, 0, (h - 4) * 128, (h - 3) * 128),
                    qtm(h * 128, (h + 1) * 128), idn[:, :]))
                p.op("t", lambda eng, h=h: eng.transpose(
                    PA(3, 1, (h - 4) * 128, (h - 3) * 128),
                    ktm(h * 128, (h + 1) * 128), idn[:, :]))
            tg2 = p.dinc("t")
            rPS[(3, 0)].w = tg2; rPS[(3, 1)].w = tg2
            p.wait("s", tg2)
            p.op("s", lambda eng: eng.copy(qhT(512, 1024), PA(3, 0)))
            p.op("s", lambda eng: eng.copy(khT(512, 1024), PA(3, 1)))
            te2 = p.dinc("s")
            rQHT.w = te2; rKHT.w = te2
            rPS[(3, 0)].r.append(te2); rPS[(3, 1)].r.append(te2)

            if _STAGE < 10:
                return
            tTA = [None] * H
            tVam = [None] * H
            tvs_last = [None]

            def emit_AT2(hp):
                p.wait("t", te1 if hp < 4 else te2)
                p.claim("t", rPS[(1, 0)], rPS[(1, 1)])
                for h in (hp, hp + 1):
                    p.op("t", lambda eng, h=h: eng.matmul(
                        PA(1, h % 2, 0, 128), khT(h * 128, (h + 1) * 128),
                        qhT(h * 128, (h + 1) * 128), start=True, stop=True))
                tTA[hp] = tTA[hp + 1] = p.dinc("t")
                rPS[(1, 0)].w = tTA[hp]; rPS[(1, 1)].w = tTA[hp]

            def emit_Am2(hp):
                p.wait("v", tTA[hp]); p.use("v", rWI)
                p.claim("v", rAM[0], rAM[1])
                for h in (hp, hp + 1):
                    p.op("v", lambda eng, h=h: eng.tensor_scalar(
                        out=am[h % 2][:, :], in0=PA(1, h % 2, 0, 128),
                        scalar1=smt[:, h:h + 1], scalar2=None, op0=ALU.mult))
                    p.op("v", lambda eng, h=h: eng.tensor_tensor(
                        out=am[h % 2][:, :], in0=am[h % 2][:, :], in1=msk[:, :],
                        op=ALU.mult))
                tVam[hp] = tVam[hp + 1] = p.dinc("v")
                rAM[0].w = rAM[1].w = tVam[hp]
                rPS[(1, 0)].r.append(tVam[hp]); rPS[(1, 1)].r.append(tVam[hp])

            def emit_oD2(hp):
                p.wait("t", tVam[hp]); p.use("t", rV, rKH, rS8)
                p.claim("t", rPS[(2, 0)], rPS[(2, 1)], rPS[(0, 0)], rPS[(0, 1)])
                for h in (hp, hp + 1):
                    p.op("t", lambda eng, h=h: eng.matmul(
                        PA(2, h % 2, 0, 128), vtm(h * 128, (h + 1) * 128),
                        am[h % 2][:, :], start=True, stop=False))
                    p.op("t", lambda eng, h=h: eng.matmul(
                        PA(2, h % 2, 0, 128), s8[:, h * 128:(h + 1) * 128],
                        qhT(h * 128, (h + 1) * 128), start=False, stop=True))
                    p.op("t", lambda eng, h=h: eng.matmul(
                        PA(0, h % 2, 0, 128), khtm(h * 128, (h + 1) * 128),
                        vtm(h * 128, (h + 1) * 128), start=True, stop=True))
                tod = p.dinc("t")
                for bk in ((2, 0), (2, 1), (0, 0), (0, 1)):
                    rPS[bk].w = tod
                rAM[0].r.append(tod); rAM[1].r.append(tod)
                p.wait("s", tod)
                for h in (hp, hp + 1):
                    p.op("s", lambda eng, h=h: eng.copy(
                        am[h % 2][:, :], PA(2, h % 2, 0, 128)))
                tso = p.dinc("s")
                rPS[(2, 0)].r.append(tso); rPS[(2, 1)].r.append(tso)
                rAM[0].w = rAM[1].w = tso
                p.wait("t", tso)
                for h in (hp, hp + 1):
                    for half in range(2):
                        p.op("t", lambda eng, h=h, half=half: eng.matmul(
                            PA(3, half), am[h % 2][:, :],
                            wo[:, h * 1024 + half * 512: h * 1024 + (half + 1) * 512],
                            start=(h == 0), stop=False))
                    tap("oT_%d_%d" % (i, h), am[h % 2][:, :], tso, (128, 128),
                        rAM[h % 2])
                if hp == H - 2:
                    for half in range(2):
                        p.op("t", lambda eng, half=half: eng.matmul(
                            PA(3, half), orow,
                            poolC[0:1, 1024 + half * 512:1024 + (half + 1) * 512],
                            start=False, stop=True))
                tWo = p.dinc("t")
                rAM[0].r.append(tWo); rAM[1].r.append(tWo)
                p.wait("v", tod); p.use("v", rDBC)
                for h in (hp, hp + 1):
                    p.op("v", lambda eng, h=h: eng.scalar_tensor_tensor(
                        out=s8[:, h * 128:(h + 1) * 128],
                        in0=s8[:, h * 128:(h + 1) * 128],
                        scalar=smt[:, 24 + h:25 + h], in1=PA(0, h % 2, 0, 128),
                        op0=ALU.mult, op1=ALU.add))
                tvs = p.dinc("v")
                rPS[(0, 0)].r.append(tvs); rPS[(0, 1)].r.append(tvs)
                tvs_last[0] = tvs

            p.claim("t", rPS[(3, 0)], rPS[(3, 1)])
            emit_AT2(0)
            for hp in range(0, H, 2):
                emit_Am2(hp)
                if hp + 2 < H:
                    emit_AT2(hp + 2)
                emit_oD2(hp)
            rS8.w = tvs_last[0]
            tap("s8_%d" % i, s8[:, :], rS8.w, (128, 1024), rS8)
            two = p.dinc("t")
            rPS[(3, 0)].w = two; rPS[(3, 1)].w = two
            if _STAGE < 11:
                return
            p.wait("v", two); p.use("v", rXB[sl])
            p.op("v", lambda eng: eng.tensor_tensor(out=x2b[:, :],
                                                    in0=xb[sl][:, :],
                                                    in1=PT[3][:, :], op=ALU.add))
            rX2.w = p.dinc("v")
            rXB[sl].r.append(rX2.w)
            rPS[(3, 0)].r.append(rX2.w); rPS[(3, 1)].r.append(rX2.w)
            tap("x2_%d" % i, x2b[:, :], rX2.w, (128, 1024), rX2)
            p.wait("g", rX2.w)
            rX2.r.append(p.dma("g", x2_hbm[i * C:(i + 1) * C, :], x2b[:, :]))

            if _STAGE < 12:
                return
            p.use("v", rX2)
            p.op("v", lambda eng: eng.bn_stats(smt[:, 40:46], x2b[:, 0:512]))
            p.op("v", lambda eng: eng.bn_stats(smt[:, 46:52], x2b[:, 512:1024]))
            p.op("v", lambda eng: eng.drain())
            p.op("v", lambda eng: eng.bn_aggr(smt[:, 52:54], smt[:, 40:52]))
            tv2 = p.dinc("v")
            p.wait("s", tv2)
            p.op("s", lambda eng: eng.activation(out=smt[:, 54:55], in_=smt[:, 53:54],
                                                 func=AF.Sqrt, bias=cst[:, 0:1],
                                                 scale=1.0))
            ts2 = p.dinc("s")
            p.wait("v", ts2)
            p.op("v", lambda eng: eng.reciprocal(smt[:, 54:55], smt[:, 54:55]))
            p.op("v", lambda eng: eng.drain())
            p.op("v", lambda eng: eng.tensor_scalar(out=ktm(), in0=x2b[:, :],
                                                    scalar1=smt[:, 52:53],
                                                    scalar2=smt[:, 54:55],
                                                    op0=ALU.subtract, op1=ALU.mult))
            p.op("v", lambda eng: eng.bn_stats(smt[:, 40:46], ktm(0, 512)))
            p.op("v", lambda eng: eng.bn_stats(smt[:, 46:52], ktm(512, 1024)))
            p.op("v", lambda eng: eng.drain())
            p.op("v", lambda eng: eng.bn_aggr(smt[:, 52:54], smt[:, 40:52]))
            tv3 = p.dinc("v")
            p.wait("s", tv3)
            p.op("s", lambda eng: eng.activation(out=smt[:, 54:55], in_=smt[:, 53:54],
                                                 func=AF.Sqrt, bias=cst[:, 0:1],
                                                 scale=1.0))
            ts3 = p.dinc("s")
            p.wait("v", ts3)
            p.op("v", lambda eng: eng.reciprocal(smt[:, 54:55], smt[:, 54:55]))
            p.op("v", lambda eng: eng.drain())
            p.op("v", lambda eng: eng.tensor_scalar(out=qtm(), in0=ktm(),
                                                    scalar1=smt[:, 52:53],
                                                    scalar2=smt[:, 54:55],
                                                    op0=ALU.subtract, op1=ALU.mult))
            p.op("v", lambda eng: eng.tensor_tensor(out=qtm(), in0=qtm(),
                                                    in1=gbg(), op=ALU.mult))
            p.op("v", lambda eng: eng.tensor_tensor(out=qtm(), in0=qtm(),
                                                    in1=gbb(), op=ALU.add))
            tv4 = p.dinc("v")
            rSTAT.w = tv4
            tap("xn3_%d" % i, qtm(), tv4, (128, 1024), rQ)

            if _STAGE < 13:
                return
            p.wait("s", tv4)
            p.op("s", lambda eng: eng.activation(out=vtm(), in_=qtm(),
                                                 func=AF.Square,
                                                 accum_out=smt[:, 32:33]))
            p.op("s", lambda eng: eng.drain())
            p.op("s", lambda eng: eng.activation(out=smt[:, 33:34], in_=smt[:, 32:33],
                                                 func=AF.Sqrt))
            ts5 = p.dinc("s")
            p.wait("v", ts5)
            p.claim("v", rX2)
            p.op("v", lambda eng: eng.tensor_scalar(out=smt[:, 33:34], in0=smt[:, 33:34],
                                                    scalar1=NORM_EPS, scalar2=None,
                                                    op0=ALU.max))
            p.op("v", lambda eng: eng.reciprocal(smt[:, 34:35], smt[:, 33:34]))
            p.op("v", lambda eng: eng.drain())
            p.op("v", lambda eng: eng.tensor_tensor(out=smt[:, 34:35], in0=smt[:, 34:35],
                                                    in1=scl[:, :], op=ALU.mult))
            p.op("v", lambda eng: eng.tensor_scalar(out=khtm(), in0=qtm(),
                                                    scalar1=smt[:, 34:35], scalar2=None,
                                                    op0=ALU.mult))
            tv5 = p.dinc("v")
            tap("xq_%d" % i, khtm(), tv5, (128, 1024), rKH)

            if _STAGE < 14:
                return
            p.wait("t", tv5)
            p.claim("t", rPS[(1, 0)], rPS[(1, 1)])
            for j in range(8):
                half, qtr = (0, j) if j < 4 else (1, j - 4)
                p.op("t", lambda eng, j=j, half=half, qtr=qtr: eng.transpose(
                    PA(1, half, qtr * 128, (qtr + 1) * 128),
                    khtm(j * 128, (j + 1) * 128), idn[:, :]))
            tt2 = p.dinc("t")
            rPS[(1, 0)].w = tt2; rPS[(1, 1)].w = tt2
            p.wait("s", tt2)
            p.claim("s", rXNT)
            p.op("s", lambda eng: eng.copy(xnT(0, 512), PA(1, 0)))
            p.op("s", lambda eng: eng.copy(xnT(512, 1024), PA(1, 1)))
            ts6 = p.dinc("s")
            rXNT.w = ts6
            rPS[(1, 0)].r.append(ts6); rPS[(1, 1)].r.append(ts6)
            rX2.r.append(ts6)

            if _STAGE < 15:
                return
            p.use("t", rXNT, rMU)
            p.claim("t", rPS[(0, 0)], rPS[(0, 1)])
            for k in range(8):
                p.op("t", lambda eng, k=k: eng.matmul(
                    PA(0, 0, 0, 128), mu[:, k * 256: k * 256 + 128],
                    xnT(k * 128, (k + 1) * 128), start=(k == 0), stop=(k == 7)))
                p.op("t", lambda eng, k=k: eng.matmul(
                    PA(0, 1, 0, 128), mu[:, k * 256 + 128: (k + 1) * 256],
                    xnT(k * 128, (k + 1) * 128), start=(k == 0), stop=(k == 7)))
            tlg = p.dinc("t")
            rPS[(0, 0)].w = tlg; rPS[(0, 1)].w = tlg
            rMU.r.append(tlg)
            p.wait("s", tlg)
            p.claim("s", rLG)
            p.op("s", lambda eng: eng.copy(lgsb[:, 0:128], PA(0, 0, 0, 128)))
            p.op("s", lambda eng: eng.copy(lgsb[:, 128:256], PA(0, 1, 0, 128)))
            tlge = p.dinc("s")
            rLG.w = tlge
            rPS[(0, 0)].r.append(tlge); rPS[(0, 1)].r.append(tlge)
            tap("lg_%d" % i, lgsb[:, :], tlge, (128, 256), rLG)
            p.wait("g", tlge)
            rLG.r.append(p.dma("g", lg_hbm[0, :, i * C:(i + 1) * C], lgsb[:, 0:128]))
            rLG.r.append(p.dma("g", lg_hbm[1, :, i * C:(i + 1) * C], lgsb[:, 128:256]))

            if _STAGE < 16:
                return
            for j in range(2):
                p.use("v", rLG, rFMZ)
                p.op("v", lambda eng, j=j: eng.reduce_max(
                    smt[:, 35:36], lgsb[:, j * 128:(j + 1) * 128], axis=AXL.X))
                p.op("v", lambda eng: eng.drain())
                p.op("v", lambda eng, j=j: eng.tensor_tensor(
                    out=smt[:, 36:37], in0=fm[:, j:j + 1], in1=smt[:, 35:36], op=ALU.max))
                p.op("v", lambda eng, j=j: eng.tensor_tensor(
                    out=smt[:, 37:38], in0=fm[:, j:j + 1], in1=smt[:, 36:37],
                    op=ALU.subtract))
                p.op("v", lambda eng, j=j: eng.tensor_copy(fm[:, j:j + 1],
                                                           smt[:, 36:37]))
                p.op("v", lambda eng: eng.tensor_scalar(
                    out=smt[:, 38:39], in0=smt[:, 36:37], scalar1=-1.0, scalar2=None,
                    op0=ALU.mult))
                tf1 = p.dinc("v")
                tap("fm_%d_%d" % (i, j), fm[:, :], tf1, (128, 2), rFMZ)
                p.wait("s", tf1)
                p.claim("s", rPSB)
                p.op("s", lambda eng: eng.activation(out=smt[:, 39:40],
                                                     in_=smt[:, 37:38], func=AF.Exp))
                p.op("s", lambda eng, j=j: eng.activation(
                    out=psb[:, 0:128],
                    in_=lgsb[:, j * 128:(j + 1) * 128],
                    func=AF.Exp, bias=smt[:, 38:39], scale=1.0,
                    accum_out=smt[:, 35:36]))
                tf2 = p.dinc("s")
                rPSB.w = tf2
                p.wait("v", tf2)
                p.op("v", lambda eng, j=j: eng.scalar_tensor_tensor(
                    out=fz[:, j:j + 1], in0=fz[:, j:j + 1], scalar=smt[:, 39:40],
                    in1=smt[:, 35:36], op0=ALU.mult, op1=ALU.add))
                tf3 = p.dinc("v")
                rFMZ.w = tf3
                tap("fz_%d_%d" % (i, j), fz[:, :], tf3, (128, 2), rFMZ)
                bk = (2, j)
                p.wait("t", tf2)
                p.claim("t", rPS[bk])
                p.op("t", lambda eng, j=j, bk=bk: eng.transpose(
                    PA(bk[0], bk[1], 0, 128), psb[:, 0:128],
                    idn[:, :]))
                tf4 = p.dinc("t")
                rPS[bk].w = tf4
                p.wait("s", tf4)
                if j == 0:
                    p.claim("s", rLG)
                p.op("s", lambda eng, j=j, bk=bk: eng.copy(
                    lgsb[:, j * 128:(j + 1) * 128], PA(bk[0], bk[1], 0, 128)))
                tf5 = p.dinc("s")
                tap("pt_%d_%d" % (i, j), lgsb[:, j * 128:(j + 1) * 128], tf5,
                    (128, 128), rLG)
                rPTM.w = tf5
                rPS[bk].r.append(tf5)
                rPSB.r.append(tf5)
                pd = 3 if j == 0 else 1
                p.wait("t", tf5)
                p.claim("t", rPS[(pd, 0)], rPS[(pd, 1)])
                for half in range(2):
                    p.op("t", lambda eng, j=j, pd=pd, half=half: eng.matmul(
                        PA(pd, half), lgsb[:, j * 128:(j + 1) * 128],
                        qtm(half * 512, (half + 1) * 512), start=True, stop=True))
                tf6 = p.dinc("t")
                rPS[(pd, 0)].w = tf6; rPS[(pd, 1)].w = tf6
                p.wait("v", tf6); p.use("v", rAACC)
                p.op("v", lambda eng, j=j, pd=pd: eng.scalar_tensor_tensor(
                    out=aacc[:, j * 1024:(j + 1) * 1024],
                    in0=aacc[:, j * 1024:(j + 1) * 1024],
                    scalar=smt[:, 39:40], in1=PT[pd][:, :], op0=ALU.mult, op1=ALU.add))
                tf7 = p.dinc("v")
                rAACC.w = tf7
                tap("aac_%d_%d" % (i, j), aacc[:, j * 1024:(j + 1) * 1024], tf7,
                    (128, 1024), rAACC)
                rPS[(pd, 0)].r.append(tf7); rPS[(pd, 1)].r.append(tf7)
                rLG.r.append(tf6)

            endtok = rAACC.w
            for r_ in (rQ, rK, rKH, rV, rQHT, rKHT):
                r_.r.append(endtok)

        for i in range(NCH if _NCHA is None else _NCHA):
            chunk_A(i)

        endA = [("sT", p.cnt["sT"]), ("sV", p.cnt["sV"]), ("sS", p.cnt["sS"])]
        for r_ in (rW1B[0], rW1B[1], rW2B[0], rW2B[1], rSOT, rMU):
            r_.r = r_.r + list(endA)
        p.claim("g", rMU)
        tb2r = p.dma("g", mu[:, 0:2048], b2r_h)
        rMU.w = tb2r

        if not _DO_B:
            p.wait("y", ("dG", p.cnt["dG"]))
            p.wait("y", ("dY", p.cnt["dY"]))
            return _realize(nc, stk, p)
        p.use("v", rAACC, rFMZ)
        p.claim("v", rAACC, rFMZ)
        p.op("v", lambda eng: eng.reciprocal(smt[:, 32:34], fz[:, :]))
        p.op("v", lambda eng: eng.drain())
        for j in range(2):
            p.op("v", lambda eng, j=j: eng.tensor_scalar(
                out=aacc[:, j * 1024:(j + 1) * 1024],
                in0=aacc[:, j * 1024:(j + 1) * 1024],
                scalar1=sm1[:, j:j + 1], scalar2=None, op0=ALU.mult))
        tB1 = p.dinc("v")
        rAACC.w = tB1
        tap("slotin", aacc[:, :], tB1, (128, 2048), rAACC)

        p.wait("t", tB1)
        p.claim("t", *[rPS[(b_, hf)] for b_ in range(4) for hf in range(2)])
        for q in range(16):
            k, j = q // 2, q % 2
            b_, hf, qt = q // 4, (q % 4) // 2, q % 2
            p.op("t", lambda eng, k=k, j=j, b_=b_, hf=hf, qt=qt: eng.transpose(
                PA(b_, hf, qt * 128, (qt + 1) * 128),
                aacc[:, j * 1024 + k * 128: j * 1024 + (k + 1) * 128], idn[:, :]))
        tB2 = p.dinc("t")
        for b_ in range(4):
            rPS[(b_, 0)].w = tB2; rPS[(b_, 1)].w = tB2
        p.wait("s", tB2)
        p.claim("s", rSIT)
        for b_ in range(4):
            for hf in range(2):
                p.op("s", lambda eng, b_=b_, hf=hf: eng.copy(
                    siT(b_ * 512 + hf * 256, b_ * 512 + hf * 256 + 256),
                    PA(b_, hf, 0, 256)))
        tB3 = p.dinc("s")
        rSIT.w = tB3
        for b_ in range(4):
            rPS[(b_, 0)].r.append(tB3); rPS[(b_, 1)].r.append(tB3)
        rAACC.r.append(tB3)
        tap("siT", siT(), tB3, (128, 2048), rSIT)

        for e in range(E):
            be = e % 2
            p.claim("g", rW1B[be], rW2B[be])
            tw1 = p.dma("g", w1b[be]().rearrange("p (k h) -> p k h", k=8),
                        w1_h[e, :, :].rearrange("(k p) h -> p k h", p=128))
            tw2 = p.dma("g", w2b[be]().rearrange("p (k d) -> p k d", k=2),
                        w2_h[e, :, :].rearrange("(k p) d -> p k d", p=128))
            rW1B[be].w = tw1; rW2B[be].w = tw2

            pb = e % 2
            p.waits("t", [tw1, tB3]); p.use("t", rSIT)
            p.claim("t", rPS[(pb, 0)], rPS[(pb, 1)])
            for m in range(2):
                for k in range(8):
                    p.op("t", lambda eng, e=e, m=m, k=k, pb=pb: eng.matmul(
                        PA(pb, m, 0, 16),
                        w1b[e % 2](k * 256 + m * 128, k * 256 + (m + 1) * 128),
                        siT(k * 256 + e * 16, k * 256 + (e + 1) * 16),
                        start=(k == 0), stop=(k == 7)))
            tBh = p.dinc("t")
            rPS[(pb, 0)].w = tBh; rPS[(pb, 1)].w = tBh
            rW1B[be].r.append(tBh); rSIT.r.append(tBh)
            p.wait("s", tBh)
            p.claim("s", rHTS)
            for m in range(2):
                p.op("s", lambda eng, e=e, m=m, pb=pb: eng.activation(
                    out=smt[:, 64 + m * 16:64 + (m + 1) * 16], in_=PA(pb, m, 0, 16),
                    func=AF.Gelu, bias=b1c[:, m * 16 + e: m * 16 + e + 1],
                    scale=1.0))
            tBg = p.dinc("s")
            rHTS.w = tBg
            rPS[(pb, 0)].r.append(tBg); rPS[(pb, 1)].r.append(tBg)

            po = 2 + e % 2
            p.waits("t", [tBg, tw2])
            p.claim("t", rPS[(po, 0)], rPS[(po, 1)])
            for m in range(2):
                for half in range(2):
                    p.op("t", lambda eng, e=e, m=m, half=half, po=po: eng.matmul(
                        PA(po, half, 0, 512, 0, 16), smt[:, 64 + m * 16:64 + (m + 1) * 16],
                        w2b[e % 2](m * 1024 + half * 512, m * 1024 + (half + 1) * 512),
                        start=(m == 0), stop=(m == 1)))
            tBs = p.dinc("t")
            rPS[(po, 0)].w = tBs; rPS[(po, 1)].w = tBs
            rW2B[be].r.append(tBs); rHTS.r.append(tBs)
            p.wait("s", tBs)
            p.claim("s", rSOT)
            p.op("s", lambda eng, po=po: eng.copy(x2b[0:16, 0:1024],
                                                  PT[po][0:16, :]))
            tBt = p.dinc("s")
            rSOT.w = tBt
            rPS[(po, 0)].r.append(tBt); rPS[(po, 1)].r.append(tBt)
            p.wait("g", tBt)
            dst = aacc[(e % 8) * 16:(e % 8) * 16 + 16,
                       (e // 8) * 1024:(e // 8) * 1024 + 1024]
            rSOT.r.append(p.dma("g", dst, x2b[0:16, 0:1024]))
        tok_so = ("dG", p.cnt["dG"])
        p.wait("v", tok_so)
        p.use("v", rMU)
        for j in range(2):
            p.op("v", lambda eng, j=j: eng.tensor_tensor(
                out=so[j](), in0=so[j](), in1=mu[:, j * 1024:(j + 1) * 1024],
                op=ALU.add))
        tok_so_v = p.dinc("v")
        rAACC.w = tok_so_v
        tap("slotout", aacc[:, :], tok_so_v, (128, 2048), rAACC)

        if not _DO_C:
            p.wait("y", ("dG", p.cnt["dG"]))
            p.wait("y", ("dY", p.cnt["dY"]))
            return _realize(nc, stk, p)
        def chunk_C(i):
            sl = i % 2
            p.wait("y", tok_so)
            p.claim("y", rX2C[sl], rW2B[0], rW2B[1])
            rX2C[sl].w = p.dma("y", x2c[sl](), x2_hbm[i * C:(i + 1) * C, :])
            p.claim("y", rLGC[sl])
            p.dma("y", lgc[sl](0, 128), lg_hbm[0, :, i * C:(i + 1) * C])
            rLGC[sl].w = p.dma("y", lgc[sl](128, 256), lg_hbm[1, :, i * C:(i + 1) * C])

            b_ = sl
            p.use("t", rLGC[sl])
            p.claim("t", rPS[(b_, 0)])
            p.op("t", lambda eng, sl=sl, b_=b_: eng.transpose(
                PA(b_, 0, 0, 128), lgc[sl](0, 128), idn[:, :]))
            p.op("t", lambda eng, sl=sl, b_=b_: eng.transpose(
                PA(b_, 0, 128, 256), lgc[sl](128, 256), idn[:, :]))
            tc1 = p.dinc("t")
            rPS[(b_, 0)].w = tc1
            p.wait("s", tc1)
            p.claim("s", rLTM, rSIT)
            p.op("s", lambda eng, b_=b_: eng.copy(ltm(0, 256), PA(b_, 0, 0, 256)))
            tc2 = p.dinc("s")
            rLTM.w = tc2
            rPS[(b_, 0)].r.append(tc2)
            rLGC[sl].r.append(tc2)
            p.use("v", rLTM)
            p.op("v", lambda eng: eng.reduce_max(smt[:, 32:33], ltm(0, 256), axis=AXL.X))
            p.op("v", lambda eng: eng.drain())
            p.op("v", lambda eng: eng.tensor_scalar(out=smt[:, 33:34], in0=smt[:, 32:33],
                                                    scalar1=-1.0, scalar2=None,
                                                    op0=ALU.mult))
            tc3 = p.dinc("v")
            p.wait("s", tc3)
            p.claim("s", rPTC)
            p.op("s", lambda eng: eng.activation(out=ptc(0, 256), in_=ltm(0, 256),
                                                 func=AF.Exp, bias=smt[:, 33:34],
                                                 scale=1.0, accum_out=smt[:, 34:35]))
            tc4 = p.dinc("s")
            rPTC.w = tc4
            rLTM.r.append(tc4)
            p.wait("v", tc4)
            p.op("v", lambda eng: eng.reciprocal(smt[:, 35:36], smt[:, 34:35]))
            p.op("v", lambda eng: eng.drain())
            p.op("v", lambda eng: eng.tensor_scalar(out=ptc(0, 256), in0=ptc(0, 256),
                                                    scalar1=smt[:, 35:36], scalar2=None,
                                                    op0=ALU.mult))
            tc5 = p.dinc("v")
            rPTC.w = tc5
            p.wait("t", tc5)
            p.claim("t", rPS[(b_, 1)])
            p.op("t", lambda eng, b_=b_: eng.transpose(PA(b_, 1, 0, 128),
                                                       ptc(0, 128), idn[:, :]))
            p.op("t", lambda eng, b_=b_: eng.transpose(PA(b_, 1, 128, 256),
                                                       ptc(128, 256), idn[:, :]))
            tc6 = p.dinc("t")
            rPS[(b_, 1)].w = tc6
            rPTC.r.append(tc6)
            p.wait("s", tc6)
            p.claim("s", rCT)
            p.op("s", lambda eng, b_=b_: eng.copy(ct(0, 256), PA(b_, 1, 0, 256)))
            tc7 = p.dinc("s")
            rCT.w = tc7
            rPS[(b_, 1)].r.append(tc7)
            po = 2 + sl
            p.use("t", rCT, rAACC)
            p.wait("t", tok_so)
            p.claim("t", rPS[(po, 0)], rPS[(po, 1)])
            for j in range(2):
                for half in range(2):
                    p.op("t", lambda eng, j=j, half=half, po=po: eng.matmul(
                        PA(po, half), ct(j * 128, (j + 1) * 128),
                        so[j](half * 512, (half + 1) * 512),
                        start=(j == 0), stop=(j == 1)))
            tc8 = p.dinc("t")
            rPS[(po, 0)].w = tc8; rPS[(po, 1)].w = tc8
            rCT.r.append(tc8)
            p.wait("v", tc8); p.use("v", rX2C[sl])
            p.claim("v", rOUTC[sl])
            p.op("v", lambda eng, sl=sl, po=po: eng.tensor_tensor(
                out=outc[sl](), in0=x2c[sl](), in1=PT[po][:, :], op=ALU.add))
            tc9 = p.dinc("v")
            rOUTC[sl].w = tc9
            rX2C[sl].r.append(tc9)
            rPS[(po, 0)].r.append(tc9); rPS[(po, 1)].r.append(tc9)
            p.wait("g", tc9)
            rOUTC[sl].r.append(p.dma("g", y_h[i * C:(i + 1) * C, :], outc[sl]()))

        for i in range(NCH):
            chunk_C(i)

        p.wait("y", ("dG", p.cnt["dG"]))
        p.wait("y", ("dY", p.cnt["dY"]))
        return _realize(nc, stk, p)


def _realize(nc, stk, p):
    block = stk.enter_context(nc.Block())

    @block.tensor
    def _(eng):
        for fn in p.stream["t"]:
            fn(eng)

    @block.vector
    def _(eng):
        for fn in p.stream["v"]:
            fn(eng)

    @block.scalar
    def _(eng):
        for fn in p.stream["s"]:
            fn(eng)

    @block.gpsimd
    def _(eng):
        for fn in p.stream["g"]:
            fn(eng)

    @block.sync
    def _(eng):
        for fn in p.stream["y"]:
            fn(eng)
    return nc


def _host_consts():
    import jax
    import jax.numpy as jnp
    half = DH // 2
    with jax.default_device(jax.devices("cpu")[0]):
        inv = 1.0 / ROPE_BASE ** (jnp.arange(half, dtype=jnp.float32) / half)
        freqs = jnp.arange(L, dtype=jnp.float32)[:, None] * inv[None, :]
        cos = np.asarray(jnp.cos(freqs), np.float32)
        sin = np.asarray(jnp.sin(freqs), np.float32)
    costab = np.empty((L, DH), np.float32)
    sintab = np.empty((L, DH), np.float32)
    costab[:, 0::2] = cos; costab[:, 1::2] = cos
    sintab[:, 0::2] = -sin; sintab[:, 1::2] = sin
    s_idx = np.arange(128)[:, None]
    t_idx = np.arange(128)[None, :]
    cstv = np.zeros((128, 4), np.float32)
    cstv[:, 0] = LN_EPS
    return {
        "costab": costab,
        "sintab": sintab,
        "identity": np.eye(128, dtype=np.float32),
        "maskut": (s_idx <= t_idx).astype(np.float32),
        "onescol": np.ones((128, 1), np.float32),
        "ones8": np.ones((H, 128), np.float32),
        "cst": cstv,
    }


def _host_inputs(L0_attn_x, L0_attn_Wq, L0_attn_Wk, L0_attn_Wv, L0_attn_Wbeta,
                 L0_attn_bbeta, L0_attn_Walpha, L0_attn_balpha, L0_attn_Wo,
                 L0_attn_bo, L0_ffn_ln_gamma, L0_ffn_ln_beta, L0_ffn_mu,
                 L0_ffn_scale, L0_ffn_w1, L0_ffn_b1, L0_ffn_w2, L0_ffn_b2):
    f = np.float32
    x = np.ascontiguousarray(np.asarray(L0_attn_x, f))
    wab = np.concatenate([np.asarray(L0_attn_Wbeta, f),
                          np.asarray(L0_attn_Walpha, f)], 0)
    b1 = np.asarray(L0_ffn_b1, f)
    b1cols = np.zeros((128, 32), f)
    for m in range(2):
        for e in range(E):
            b1cols[:, m * 16 + e] = b1[e, m * 128:(m + 1) * 128]
    shared = {
        "wqT": np.ascontiguousarray(np.asarray(L0_attn_Wq, f).T),
        "wkT": np.ascontiguousarray(np.asarray(L0_attn_Wk, f).T),
        "wvT": np.ascontiguousarray(np.asarray(L0_attn_Wv, f).T),
        "woT": np.ascontiguousarray(np.asarray(L0_attn_Wo, f).T),
        "wabT": np.ascontiguousarray(wab.T),
        "boflat": np.ascontiguousarray(np.asarray(L0_attn_bo, f).reshape(1, D)),
        "muflat": np.ascontiguousarray(np.asarray(L0_ffn_mu, f).reshape(D, ES)),
        "gbg": np.ascontiguousarray(
            np.broadcast_to(np.asarray(L0_ffn_ln_gamma, f), (128, D))),
        "gbb": np.ascontiguousarray(
            np.broadcast_to(np.asarray(L0_ffn_ln_beta, f), (128, D))),
        "sclbc": np.ascontiguousarray(np.broadcast_to(
            np.asarray(L0_ffn_scale, f).reshape(1, 1), (128, 1))),
        "b1cols": b1cols,
        "b2rep": np.ascontiguousarray(np.concatenate(
            [np.repeat(np.asarray(L0_ffn_b2, f), NS, axis=0)[0:128],
             np.repeat(np.asarray(L0_ffn_b2, f), NS, axis=0)[128:256]], axis=1)),
        "w1": np.ascontiguousarray(np.asarray(L0_ffn_w1, f)),
        "w2": np.ascontiguousarray(np.asarray(L0_ffn_w2, f)),
    }
    shared.update(_host_consts())
    cstv = shared["cst"].copy()
    cstv[0:8, 1] = np.asarray(L0_attn_bbeta, f)
    cstv[0:8, 2] = np.asarray(L0_attn_balpha, f)
    shared["cst"] = cstv
    in_maps = []
    for b in range(B):
        m = dict(shared)
        m["x"] = np.ascontiguousarray(x[b])
        in_maps.append(m)
    return in_maps


_NC = None


def kernel(**inputs):
    global _NC
    if _NC is None:
        _NC = _build()
    in_maps = _host_inputs(**inputs)
    res = run_bass_kernel_spmd(_NC, in_maps, core_ids=list(range(B)))
    out = np.stack([np.asarray(res.results[b]["y"]) for b in range(B)], axis=0)
    return np.ascontiguousarray(out.astype(np.float32))


# version 06 (final)
# speedup vs baseline: 1.0395x; 1.0146x over previous
import numpy as np
from contextlib import ExitStack

import concourse.bass as bass
import concourse.mybir as mybir
from concourse.bass_utils import run_bass_kernel_spmd

F32 = mybir.dt.float32
AF = mybir.ActivationFunctionType
ALU = mybir.AluOpType
AXL = mybir.AxisListType

B, L, D = 8, 4096, 1024
H, DH = 8, 128
C = 128
NCH = L // C
E, NS, HID, ES = 16, 16, 256, 256
LN_EPS = 1e-5
NORM_EPS = 1e-12
ROPE_BASE = 10000.0

_STAGE = 99
_S6 = 9
_DO_B = True
_DO_C = True
_NCHA = None

_BSPAN = 0.999 - 0.85
_D_HI = float(np.float32(_BSPAN)); _D_LO = float(np.float32(_BSPAN - _D_HI))
_B_HI = float(np.float32(0.85));   _B_LO = float(np.float32(0.85 - _B_HI))


class _Res:
    __slots__ = ("w", "r")

    def __init__(self):
        self.w = None
        self.r = []


class _Seg:
    def __init__(self, h, off, width):
        self.h, self.off, self.wd = h, off, width

    def __call__(self, a=0, b=None):
        b = self.wd if b is None else b
        return self.h[:, self.off + a:self.off + b]


class _P:
    ESEM = {"t": "sT", "v": "sV", "s": "sS"}
    QSEM = {"y": "dY", "g": "dG"}

    def __init__(self):
        self.stream = {e: [] for e in "tvsgy"}
        self.cnt = {n: 0 for n in ("sT", "sV", "sS", "dY", "dG")}
        self.seen = {}
        self.sems = None

    def wait(self, e, tok):
        if tok is None:
            return
        s, v = tok
        if self.seen.get((e, s), 0) >= v:
            return
        self.seen[(e, s)] = v
        self.stream[e].append(lambda eng, s=s, v=v: eng.wait_ge(self.sems[s], v))

    def waits(self, e, toks):
        for t in toks:
            self.wait(e, t)

    def use(self, e, *ress):
        for r in ress:
            self.wait(e, r.w)

    def claim(self, e, *ress):
        for r in ress:
            for t in r.r:
                self.wait(e, t)
            r.r = []

    def op(self, e, fn):
        self.stream[e].append(fn)

    def dinc(self, e):
        s = self.ESEM[e]
        self.cnt[s] += 1
        v = self.cnt[s]
        self.stream[e].append(lambda eng, s=s: eng.drain().then_inc(self.sems[s], 1))
        return (s, v)

    def dma(self, e, out, in_):
        s = self.QSEM[e]
        self.cnt[s] += 16
        v = self.cnt[s]
        self.stream[e].append(
            lambda eng, o=out, i=in_: eng.dma_start(out=o, in_=i).then_inc(self.sems[s], 16))
        return (s, v)


def _build(taps=()):
    taps = set(taps)
    nc = bass.Bass("TRN2", target_bir_lowering=False, detect_race_conditions=False)

    dt = nc.dram_tensor
    x_h = dt("x", [L, D], F32, kind="ExternalInput").ap()
    wq_h = dt("wqT", [D, D], F32, kind="ExternalInput").ap()
    wk_h = dt("wkT", [D, D], F32, kind="ExternalInput").ap()
    wv_h = dt("wvT", [D, D], F32, kind="ExternalInput").ap()
    wo_h = dt("woT", [D, D], F32, kind="ExternalInput").ap()
    wab_h = dt("wabT", [D, 16], F32, kind="ExternalInput").ap()
    bo_h = dt("boflat", [1, D], F32, kind="ExternalInput").ap()
    mu_h = dt("muflat", [D, ES], F32, kind="ExternalInput").ap()
    gbg_h = dt("gbg", [128, D], F32, kind="ExternalInput").ap()
    gbb_h = dt("gbb", [128, D], F32, kind="ExternalInput").ap()
    scl_h = dt("sclbc", [128, 1], F32, kind="ExternalInput").ap()
    b1c_h = dt("b1cols", [128, 32], F32, kind="ExternalInput").ap()
    b2r_h = dt("b2rep", [128, 2 * D], F32, kind="ExternalInput").ap()
    w1_h = dt("w1", [E, D, HID], F32, kind="ExternalInput").ap()
    w2_h = dt("w2", [E, HID, D], F32, kind="ExternalInput").ap()
    cos_h = dt("costab", [L, DH], F32, kind="ExternalInput").ap()
    sin_h = dt("sintab", [L, DH], F32, kind="ExternalInput").ap()
    idn_h = dt("identity", [128, 128], F32, kind="ExternalInput").ap()
    msk_h = dt("maskut", [128, 128], F32, kind="ExternalInput").ap()
    oc_h = dt("onescol", [128, 1], F32, kind="ExternalInput").ap()
    o8_h = dt("ones8", [H, 128], F32, kind="ExternalInput").ap()
    cst_h = dt("cst", [128, 4], F32, kind="ExternalInput").ap()

    x2_hbm = dt("x2scr", [L, D], F32).ap()
    lg_hbm = dt("lgscr", [2, 128, L], F32).ap()
    y_h = dt("y", [L, D], F32, kind="ExternalOutput").ap()

    with ExitStack() as stk:
        def sb(name, shape):
            return stk.enter_context(nc.sbuf_tensor(name, shape, F32))

        wq = sb("wq", [128, 8 * D]); wk = sb("wk", [128, 8 * D])
        wv = sb("wv", [128, 8 * D]); wo = sb("wo", [128, 8 * D])
        wab = sb("wab", [128, 8 * 16])
        mu = sb("mu", [128, 8 * ES])
        idn = sb("idn", [128, 128]); msk = sb("msk", [128, 128])
        ocol = sb("ocol", [128, 1])
        o8 = sb("o8s", [H, 128])
        orow = o8[0:1, 0:128]
        b1c = sb("b1cs", [128, 32])
        scl = sb("scls", [128, 1]); cst = sb("csts", [128, 4])

        poolA = sb("poolA", [128, 4096])
        qtm = _Seg(poolA, 0, 1024)
        ktm = _Seg(poolA, 1024, 1024)
        vtm = _Seg(poolA, 2048, 1024)
        gbg = _Seg(poolA, 3072, 1024)
        w1b = [_Seg(poolA, 0, 2048), _Seg(poolA, 2048, 2048)]

        poolB = sb("poolB", [128, 4096])
        qhT = _Seg(poolB, 0, 1024)
        khT = _Seg(poolB, 1024, 1024)
        xn1 = _Seg(poolB, 2048, 1024)
        gbb = _Seg(poolB, 3072, 1024)
        w2b = [_Seg(poolB, 0, 2048), _Seg(poolB, 2048, 2048)]
        x2c = [_Seg(poolB, 0, 1024), _Seg(poolB, 1024, 1024)]
        outc = [_Seg(poolB, 2048, 1024), _Seg(poolB, 3072, 1024)]

        poolC = sb("poolC", [128, 2048])
        xnT = _Seg(poolC, 0, 1024)
        siT = _Seg(poolC, 0, 2048)
        ltm = _Seg(poolC, 0, 256)
        ptc = _Seg(poolC, 256, 256)
        ct = _Seg(poolC, 512, 256)
        lgc = [_Seg(poolC, 1024, 256), _Seg(poolC, 1280, 256)]

        xb = [sb("xb0", [128, D]), sb("xb1", [128, D])]
        csb = sb("csb", [128, 256])
        x2b = sb("x2b", [128, D])
        swp = _Seg(x2b, 0, 1024)
        khtm = swp
        am = [sb("am0", [128, 128]), sb("am1", [128, 128])]
        lgsb = sb("lgsb", [128, 256])
        psb = sb("psb", [128, 128])
        aacc = sb("aacc", [128, 2 * D])
        so = [_Seg(aacc, 0, 1024), _Seg(aacc, 1024, 1024)]
        s8 = sb("s8", [128, 8 * 128])
        fm = sb("fm", [128, 2]); fz = sb("fz", [128, 2])
        sg16 = sb("sg16", [8, 256])
        smt = sb("smt", [128, 96])
        witm = smt[:, 0:8]; dtm = smt[:, 8:16]; wstm = smt[:, 16:24]
        dbc = smt[:, 24:32]; sm1 = smt[:, 32:40]; st6 = smt[:, 40:52]
        mv = smt[:, 52:54]; rstd = smt[:, 54:55]; dlT = smt[0:1, 56:64]
        hts = smt[:, 64:96]

        PT = [stk.enter_context(nc.psum_tensor("p%d" % i, [128, 1024], F32))
              for i in range(4)]

        bb1 = am[0][0:8, 0:128]
        bb2 = am[1][0:8, 0:128]
        dhm = psb[0:8, 0:128]
        wih = am[0][0:8, 0:128]
        wsh = am[1][0:8, 0:128]
        mun = lgsb[0:1, 0:256]
        bo = poolC[0:1, 1024:2048]

        sems = {n: stk.enter_context(nc.semaphore(n))
                for n in ("sT", "sV", "sS", "dY", "dG")}

        p = _P()
        p.sems = sems

        rPS = {(i, hf): _Res() for i in range(4) for hf in range(2)}
        rXB = [_Res(), _Res()]; rCS = _Res(); rX2 = _Res()
        rXN1 = _Res(); rXNT = _Res(); rQ = _Res(); rK = _Res(); rKH = _Res()
        rV = _Res(); rQHT = _Res(); rKHT = _Res(); rAM = [_Res(), _Res()]
        rSG = _Res(); rAB = _Res(); rWI = _Res(); rDBC = _Res()
        rS8 = _Res(); rAACC = _Res(); rFMZ = _Res(); rLG = _Res(); rPSB = _Res()
        rPTM = _Res(); rMU = _Res(); rSTAT = _Res()
        rW1B = [_Res(), _Res()]; rW2B = [_Res(), _Res()]; rHTS = _Res()
        rSOT = _Res(); rSIT = _Res()
        rLGC = [_Res(), _Res()]; rX2C = [_Res(), _Res()]; rOUTC = [_Res(), _Res()]
        rLTM = _Res(); rPTC = _Res(); rCT = _Res()

        def PA(i, half, a=0, b=512, pp0=0, pp1=128):
            return PT[i][pp0:pp1, half * 512 + a: half * 512 + b]

        def tap(name, ap, tok, shape, res=None):
            if name not in taps:
                return
            out = nc.dram_tensor("tap_" + name, list(shape), F32,
                                 kind="ExternalOutput").ap()
            p.wait("g", tok)
            dtok = p.dma("g", out, ap)
            if res is not None:
                res.r.append(dtok)

        for dst, src in (
            (wq[:, :].rearrange("p (k n) -> p k n", k=8),
             wq_h.rearrange("(k p) n -> p k n", p=128)),
            (wk[:, :].rearrange("p (k n) -> p k n", k=8),
             wk_h.rearrange("(k p) n -> p k n", p=128)),
            (wv[:, :].rearrange("p (k n) -> p k n", k=8),
             wv_h.rearrange("(k p) n -> p k n", p=128)),
            (wo[:, :].rearrange("p (k n) -> p k n", k=8),
             wo_h.rearrange("(k p) n -> p k n", p=128)),
            (wab[:, :].rearrange("p (k n) -> p k n", k=8),
             wab_h.rearrange("(k p) n -> p k n", p=128)),
            (mu[:, :].rearrange("p (k n) -> p k n", k=8),
             mu_h.rearrange("(k p) n -> p k n", p=128)),
            (gbg(), gbg_h), (gbb(), gbb_h),
            (idn[:, :], idn_h), (msk[:, :], msk_h),
            (ocol[:, :], oc_h),
            (o8[:, :], o8_h),
            (bo, bo_h),
            (b1c[:, :], b1c_h),
            (scl[:, :], scl_h), (cst[:, :], cst_h),
        ):
            tok_lds = p.dma("y", dst, src)

        p.wait("v", tok_lds)
        p.op("v", lambda eng: eng.memset(s8[:, :], 0.0))
        p.op("v", lambda eng: eng.memset(fm[:, :], -1e30))
        p.op("v", lambda eng: eng.memset(fz[:, :], 0.0))
        p.op("v", lambda eng: eng.memset(aacc[:, :], 0.0))
        t0 = p.dinc("v")
        rS8.w = t0; rFMZ.w = t0; rAACC.w = t0

        p.wait("s", tok_lds)
        p.op("s", lambda eng: eng.activation(out=xn1(), in_=mu[:, 0:1024],
                                             func=AF.Square))
        p.op("s", lambda eng: eng.activation(out=swp(), in_=mu[:, 1024:2048],
                                             func=AF.Square))
        tmsq = p.dinc("s")
        p.wait("t", tmsq)
        for k in range(8):
            src = xn1(k * 256, (k + 1) * 256) if k < 4 else \
                swp((k - 4) * 256, (k - 3) * 256)
            p.op("t", lambda eng, s=src, k=k: eng.matmul(
                PA(0, 0, 0, 256, 0, 1), ocol[:, :], s, start=(k == 0), stop=(k == 7)))
        tcs = p.dinc("t")
        rPS[(0, 0)].w = tcs
        rXN1.r.append(tcs)
        p.wait("s", tcs)
        p.op("s", lambda eng: eng.activation(out=mun, in_=PA(0, 0, 0, 256, 0, 1),
                                             func=AF.Sqrt))
        tsq = p.dinc("s")
        rPS[(0, 0)].r.append(tsq)
        p.wait("v", tsq)
        p.op("v", lambda eng: eng.tensor_scalar(out=mun, in0=mun,
                                                scalar1=NORM_EPS, scalar2=None,
                                                op0=ALU.max))
        p.op("v", lambda eng: eng.reciprocal(mun, mun))
        trn = p.dinc("v")
        p.wait("t", trn)
        p.claim("t", rPS[(0, 0)])
        p.op("t", lambda eng: eng.matmul(PA(0, 0, 0, 256, 0, 128), orow,
                                         mun, start=True, stop=True))
        tbc = p.dinc("t")
        rPS[(0, 0)].w = tbc
        p.wait("v", tbc)
        for k in range(8):
            p.op("v", lambda eng, k=k: eng.tensor_tensor(
                out=mu[:, k * 256:(k + 1) * 256], in0=mu[:, k * 256:(k + 1) * 256],
                in1=PA(0, 0, 0, 256), op=ALU.mult))
        tmu = p.dinc("v")
        rMU.w = tmu
        rPS[(0, 0)].r.append(tmu)

        def chunk_A(i):
            sl = i % 2
            p.claim("y", rXB[sl])
            rXB[sl].w = p.dma("y", xb[sl][:, :], x_h[i * C:(i + 1) * C, :])
            p.claim("y", rCS)
            p.dma("y", csb[:, 0:128], cos_h[i * C:(i + 1) * C, :])
            rCS.w = p.dma("y", csb[:, 128:256], sin_h[i * C:(i + 1) * C, :])

            p.use("v", rXB[sl]); p.claim("v", rSTAT)
            p.op("v", lambda eng: eng.bn_stats(smt[:, 40:46], xb[sl][:, 0:512]))
            p.op("v", lambda eng: eng.bn_stats(smt[:, 46:52], xb[sl][:, 512:1024]))
            p.op("v", lambda eng: eng.drain())
            p.op("v", lambda eng: eng.bn_aggr(smt[:, 52:54], smt[:, 40:52]))
            tv = p.dinc("v")
            p.wait("s", tv)
            p.op("s", lambda eng: eng.activation(out=smt[:, 54:55], in_=smt[:, 53:54],
                                                 func=AF.Sqrt, bias=cst[:, 0:1],
                                                 scale=1.0))
            ts_ = p.dinc("s")
            p.wait("v", ts_)
            p.op("v", lambda eng: eng.reciprocal(smt[:, 54:55], smt[:, 54:55]))
            p.op("v", lambda eng: eng.drain())
            p.claim("v", rXN1)
            p.op("v", lambda eng: eng.tensor_scalar(out=xn1(), in0=xb[sl][:, :],
                                                    scalar1=smt[:, 52:53],
                                                    scalar2=smt[:, 54:55],
                                                    op0=ALU.subtract, op1=ALU.mult))
            rXN1.w = p.dinc("v")
            rSTAT.w = rXN1.w
            tap("xb_%d" % i, xb[sl][:, :], rXN1.w, (128, 1024), rXB[sl])
            tap("mv_%d" % i, smt[:, 52:54], rXN1.w, (128, 2), rSTAT)
            tap("xn1_%d" % i, xn1(), rXN1.w, (128, 1024), rXN1)

            if _STAGE < 2:
                return
            p.use("t", rXN1)
            p.claim("t", rPS[(3, 0)], rPS[(3, 1)])
            for j in range(8):
                half, qtr = (0, j) if j < 4 else (1, j - 4)
                p.op("t", lambda eng, j=j, half=half, qtr=qtr: eng.transpose(
                    PA(3, half, qtr * 128, (qtr + 1) * 128),
                    xn1(j * 128, (j + 1) * 128), idn[:, :]))
            tt = p.dinc("t")
            rPS[(3, 0)].w = tt; rPS[(3, 1)].w = tt
            p.wait("s", tt)
            p.claim("s", rXNT)
            p.op("s", lambda eng: eng.copy(xnT(0, 512), PA(3, 0)))
            p.op("s", lambda eng: eng.copy(xnT(512, 1024), PA(3, 1)))
            rXNT.w = p.dinc("s")
            rPS[(3, 0)].r.append(rXNT.w); rPS[(3, 1)].r.append(rXNT.w)
            rXN1.r.append(rXNT.w)

            if _STAGE < 3:
                return
            p.use("t", rXNT)
            p.claim("t", rPS[(0, 0)], rPS[(0, 1)], rPS[(1, 0)], rPS[(1, 1)],
                    rPS[(2, 0)], rPS[(2, 1)], rPS[(3, 0)], rPS[(3, 1)])
            for k in range(8):
                for (pi, w_) in ((0, wq), (1, wk), (2, wv)):
                    for half in range(2):
                        p.op("t", lambda eng, pi=pi, w_=w_, half=half, k=k:
                             eng.matmul(PA(pi, half), xnT(k * 128, (k + 1) * 128),
                                        w_[:, k * 1024 + half * 512:
                                           k * 1024 + (half + 1) * 512],
                                        start=(k == 0), stop=(k == 7)))
                p.op("t", lambda eng, k=k: eng.matmul(
                    PA(3, 0, 0, 128, 0, 8), wab[:, k * 16:k * 16 + 8],
                    xnT(k * 128, (k + 1) * 128), start=(k == 0), stop=(k == 7)))
                p.op("t", lambda eng, k=k: eng.matmul(
                    PA(3, 1, 0, 128, 0, 8), wab[:, k * 16 + 8:(k + 1) * 16],
                    xnT(k * 128, (k + 1) * 128), start=(k == 0), stop=(k == 7)))
            tproj = p.dinc("t")
            for key in ((0, 0), (0, 1), (1, 0), (1, 1), (2, 0), (2, 1),
                        (3, 0), (3, 1)):
                rPS[key].w = tproj

            if _STAGE < 4:
                return
            p.wait("s", tproj)
            p.claim("s", rSG, rV)
            p.op("s", lambda eng: eng.activation(out=sg16[0:8, 0:128],
                                                 in_=PA(3, 0, 0, 128, 0, 8),
                                                 func=AF.Sigmoid, bias=cst[0:8, 1:2],
                                                 scale=1.0))
            p.op("s", lambda eng: eng.activation(out=sg16[0:8, 128:256],
                                                 in_=PA(3, 1, 0, 128, 0, 8),
                                                 func=AF.Sigmoid, bias=cst[0:8, 2:3],
                                                 scale=1.0))
            p.op("s", lambda eng: eng.copy(vtm(0, 512), PA(2, 0)))
            p.op("s", lambda eng: eng.copy(vtm(512, 1024), PA(2, 1)))
            tvv = p.dinc("s")
            rSG.w = tvv; rV.w = tvv
            rPS[(3, 0)].r.append(tvv); rPS[(3, 1)].r.append(tvv)
            rPS[(2, 0)].r.append(tvv); rPS[(2, 1)].r.append(tvv)
            tap("vtm_%d" % i, vtm(), tvv, (128, 1024), rV)

            if _STAGE < 5:
                return
            p.use("v", rSG); p.claim("v", rAB)
            p.op("v", lambda eng: eng.tensor_scalar(out=bb1, in0=sg16[0:8, 0:128],
                                                    scalar1=_D_HI, scalar2=_B_HI,
                                                    op0=ALU.mult, op1=ALU.add))
            p.op("v", lambda eng: eng.tensor_scalar(out=bb2, in0=sg16[0:8, 0:128],
                                                    scalar1=_D_LO, scalar2=_B_LO,
                                                    op0=ALU.mult, op1=ALU.add))
            p.op("v", lambda eng: eng.tensor_tensor(out=bb1, in0=bb1,
                                                    in1=bb2, op=ALU.add))
            tb = p.dinc("v")
            p.wait("s", tb)
            p.op("s", lambda eng: eng.activation(out=bb2, in_=bb1,
                                                 func=AF.Ln))
            tl = p.dinc("s")
            p.wait("v", tl)
            p.op("v", lambda eng: eng.tensor_tensor_scan(dhm, o8[:, :],
                                                         bb2, 0.0,
                                                         ALU.mult, ALU.add))
            tc = p.dinc("v")
            p.wait("s", tc)
            p.op("s", lambda eng: eng.activation(out=dhm, in_=dhm,
                                                 func=AF.Exp))
            td = p.dinc("s")
            p.wait("v", td)
            p.op("v", lambda eng: eng.reciprocal(wih, dhm))
            p.op("v", lambda eng: eng.drain())
            p.op("v", lambda eng: eng.tensor_tensor(out=wih, in0=sg16[0:8, 128:256],
                                                    in1=wih, op=ALU.mult))
            p.op("v", lambda eng: eng.tensor_scalar(out=wsh, in0=wih,
                                                    scalar1=psb[0:8, 127:128],
                                                    scalar2=None, op0=ALU.mult))
            tw = p.dinc("v")
            rAB.w = tw
            rSG.r.append(tw)
            tap("dhm_%d" % i, dhm, tw, (8, 128), rAB)
            tap("wih_%d" % i, wih, tw, (8, 128), rAM[0])

            if _STAGE < 6:
                return
            p.wait("t", tw)
            p.claim("t", rPS[(3, 1)])
            if _S6 >= 1:
                p.op("t", lambda eng: eng.transpose(PA(3, 1, 0, 8), wih,
                                                    idn[0:8, 0:8]))
            if _S6 >= 2:
                p.op("t", lambda eng: eng.transpose(PA(3, 1, 8, 16), dhm,
                                                    idn[0:8, 0:8]))
            if _S6 >= 3:
                p.op("t", lambda eng: eng.transpose(PA(3, 1, 16, 24), wsh,
                                                    idn[0:8, 0:8]))
            if _S6 >= 4:
                p.op("t", lambda eng: eng.transpose(PA(3, 1, 24, 32, 0, 1),
                                                    psb[0:8, 127:128], idn[0:8, 0:8]))
            t1 = p.dinc("t")
            rPS[(3, 1)].w = t1
            p.wait("s", t1)
            p.claim("s", rWI)
            if _S6 >= 1:
                p.op("s", lambda eng: eng.copy(witm, PA(3, 1, 0, 8)))
            if _S6 >= 2:
                p.op("s", lambda eng: eng.copy(dtm, PA(3, 1, 8, 16)))
            if _S6 >= 3:
                p.op("s", lambda eng: eng.copy(wstm, PA(3, 1, 16, 24)))
            if _S6 >= 4:
                p.op("s", lambda eng: eng.copy(dlT, PA(3, 1, 24, 32, 0, 1)))
            t2 = p.dinc("s")
            rWI.w = t2
            p.wait("t", t2)
            if _S6 >= 5:
                p.op("t", lambda eng: eng.matmul(PA(3, 1, 32, 40), orow, dlT,
                                                 start=True, stop=True))
            t3 = p.dinc("t")
            p.wait("s", t3)
            p.claim("s", rDBC)
            if _S6 >= 6:
                p.op("s", lambda eng: eng.copy(dbc, PA(3, 1, 32, 40)))
            t4 = p.dinc("s")
            rDBC.w = t4
            rPS[(3, 1)].r.append(t4)

            if _STAGE < 7:
                return
            cosb = bass.AP(csb, 0, [[256, 128], [0, 8], [1, 128]])
            sinb = bass.AP(csb, 128, [[256, 128], [0, 8], [1, 128]])
            swp_ev = bass.AP(x2b, 0, [[1024, 128], [2, 512]])
            swp_od = bass.AP(x2b, 1, [[1024, 128], [2, 512]])
            p.wait("v", tproj); p.use("v", rCS)
            p.claim("v", rQ, rX2)
            p.op("v", lambda eng: eng.tensor_copy(
                swp_od, bass.AP(PT[0], 0, [[1024, 128], [2, 512]])))
            p.op("v", lambda eng: eng.tensor_copy(
                swp_ev, bass.AP(PT[0], 1, [[1024, 128], [2, 512]])))
            p.op("v", lambda eng: eng.tensor_tensor(out=qtm(), in0=PT[0][:, :],
                                                    in1=cosb, op=ALU.mult))
            p.op("v", lambda eng: eng.tensor_tensor(out=swp(), in0=swp(), in1=sinb,
                                                    op=ALU.mult))
            p.op("v", lambda eng: eng.tensor_tensor(out=qtm(), in0=qtm(), in1=swp(),
                                                    op=ALU.add))
            p.wait("v", t2)
            for h in range(H):
                p.op("v", lambda eng, h=h: eng.tensor_scalar(
                    out=qtm(h * 128, (h + 1) * 128), in0=qtm(h * 128, (h + 1) * 128),
                    scalar1=smt[:, 8 + h:9 + h], scalar2=None, op0=ALU.mult))
            rQ.w = p.dinc("v")
            rPS[(0, 0)].r.append(rQ.w); rPS[(0, 1)].r.append(rQ.w)
            rXN1.r.append(rQ.w)
            tap("qhat_%d" % i, qtm(), rQ.w, (128, 1024), rQ)

            if _STAGE < 8:
                return
            p.claim("v", rK, rKH)
            p.op("v", lambda eng: eng.tensor_copy(
                swp_od, bass.AP(PT[1], 0, [[1024, 128], [2, 512]])))
            p.op("v", lambda eng: eng.tensor_copy(
                swp_ev, bass.AP(PT[1], 1, [[1024, 128], [2, 512]])))
            p.op("v", lambda eng: eng.tensor_tensor(out=ktm(), in0=PT[1][:, :],
                                                    in1=cosb, op=ALU.mult))
            p.op("v", lambda eng: eng.tensor_tensor(out=swp(), in0=swp(), in1=sinb,
                                                    op=ALU.mult))
            p.op("v", lambda eng: eng.tensor_tensor(out=ktm(), in0=ktm(), in1=swp(),
                                                    op=ALU.add))
            for h in range(H):
                p.op("v", lambda eng, h=h: eng.tensor_scalar(
                    out=khtm(h * 128, (h + 1) * 128), in0=ktm(h * 128, (h + 1) * 128),
                    scalar1=smt[:, 16 + h:17 + h], scalar2=None, op0=ALU.mult))
            tk = p.dinc("v")
            rK.w = tk; rKH.w = tk
            rPS[(1, 0)].r.append(tk); rPS[(1, 1)].r.append(tk)
            rCS.r.append(tk)
            tap("krot_%d" % i, ktm(), tk, (128, 1024), rK)
            tap("khat_%d" % i, khtm(), tk, (128, 1024), rKH)

            if _STAGE < 9:
                return
            p.use("t", rQ, rK)
            p.claim("t", rPS[(1, 0)], rPS[(1, 1)])
            for h in range(4):
                p.op("t", lambda eng, h=h: eng.transpose(
                    PA(1, 0, h * 128, (h + 1) * 128), qtm(h * 128, (h + 1) * 128),
                    idn[:, :]))
                p.op("t", lambda eng, h=h: eng.transpose(
                    PA(1, 1, h * 128, (h + 1) * 128), ktm(h * 128, (h + 1) * 128),
                    idn[:, :]))
            tg1 = p.dinc("t")
            rPS[(1, 0)].w = tg1; rPS[(1, 1)].w = tg1
            p.wait("s", tg1)
            p.claim("s", rQHT, rKHT)
            p.op("s", lambda eng: eng.copy(qhT(0, 512), PA(1, 0)))
            p.op("s", lambda eng: eng.copy(khT(0, 512), PA(1, 1)))
            te1 = p.dinc("s")
            rPS[(1, 0)].r.append(te1); rPS[(1, 1)].r.append(te1)
            p.claim("t", rPS[(3, 0)], rPS[(3, 1)])
            for h in range(4, 8):
                p.op("t", lambda eng, h=h: eng.transpose(
                    PA(3, 0, (h - 4) * 128, (h - 3) * 128),
                    qtm(h * 128, (h + 1) * 128), idn[:, :]))
                p.op("t", lambda eng, h=h: eng.transpose(
                    PA(3, 1, (h - 4) * 128, (h - 3) * 128),
                    ktm(h * 128, (h + 1) * 128), idn[:, :]))
            tg2 = p.dinc("t")
            rPS[(3, 0)].w = tg2; rPS[(3, 1)].w = tg2
            p.wait("s", tg2)
            p.op("s", lambda eng: eng.copy(qhT(512, 1024), PA(3, 0)))
            p.op("s", lambda eng: eng.copy(khT(512, 1024), PA(3, 1)))
            te2 = p.dinc("s")
            rQHT.w = te2; rKHT.w = te2
            rPS[(3, 0)].r.append(te2); rPS[(3, 1)].r.append(te2)

            if _STAGE < 10:
                return
            tTA = [None] * H
            tVam = [None] * H
            tvs_last = [None]

            def emit_AT2(hp):
                p.wait("t", te1 if hp < 4 else te2)
                p.claim("t", rPS[(1, 0)], rPS[(1, 1)])
                for h in (hp, hp + 1):
                    p.op("t", lambda eng, h=h: eng.matmul(
                        PA(1, h % 2, 0, 128), khT(h * 128, (h + 1) * 128),
                        qhT(h * 128, (h + 1) * 128), start=True, stop=True))
                tTA[hp] = tTA[hp + 1] = p.dinc("t")
                rPS[(1, 0)].w = tTA[hp]; rPS[(1, 1)].w = tTA[hp]

            def emit_Am2(hp):
                p.wait("v", tTA[hp]); p.use("v", rWI)
                p.claim("v", rAM[0], rAM[1])
                for h in (hp, hp + 1):
                    p.op("v", lambda eng, h=h: eng.tensor_scalar(
                        out=am[h % 2][:, :], in0=PA(1, h % 2, 0, 128),
                        scalar1=smt[:, h:h + 1], scalar2=None, op0=ALU.mult))
                    p.op("v", lambda eng, h=h: eng.tensor_tensor(
                        out=am[h % 2][:, :], in0=am[h % 2][:, :], in1=msk[:, :],
                        op=ALU.mult))
                tVam[hp] = tVam[hp + 1] = p.dinc("v")
                rAM[0].w = rAM[1].w = tVam[hp]
                rPS[(1, 0)].r.append(tVam[hp]); rPS[(1, 1)].r.append(tVam[hp])

            def emit_oD2(hp):
                p.wait("t", tVam[hp]); p.use("t", rV, rKH, rS8)
                p.claim("t", rPS[(2, 0)], rPS[(2, 1)], rPS[(0, 0)], rPS[(0, 1)])
                for h in (hp, hp + 1):
                    p.op("t", lambda eng, h=h: eng.matmul(
                        PA(2, h % 2, 0, 128), vtm(h * 128, (h + 1) * 128),
                        am[h % 2][:, :], start=True, stop=False))
                    p.op("t", lambda eng, h=h: eng.matmul(
                        PA(2, h % 2, 0, 128), s8[:, h * 128:(h + 1) * 128],
                        qhT(h * 128, (h + 1) * 128), start=False, stop=True))
                    p.op("t", lambda eng, h=h: eng.matmul(
                        PA(0, h % 2, 0, 128), khtm(h * 128, (h + 1) * 128),
                        vtm(h * 128, (h + 1) * 128), start=True, stop=True))
                tod = p.dinc("t")
                for bk in ((2, 0), (2, 1), (0, 0), (0, 1)):
                    rPS[bk].w = tod
                rAM[0].r.append(tod); rAM[1].r.append(tod)
                p.wait("s", tod)
                for h in (hp, hp + 1):
                    p.op("s", lambda eng, h=h: eng.copy(
                        am[h % 2][:, :], PA(2, h % 2, 0, 128)))
                tso = p.dinc("s")
                rPS[(2, 0)].r.append(tso); rPS[(2, 1)].r.append(tso)
                rAM[0].w = rAM[1].w = tso
                p.wait("t", tso)
                for h in (hp, hp + 1):
                    for half in range(2):
                        p.op("t", lambda eng, h=h, half=half: eng.matmul(
                            PA(3, half), am[h % 2][:, :],
                            wo[:, h * 1024 + half * 512: h * 1024 + (half + 1) * 512],
                            start=(h == 0), stop=False))
                    tap("oT_%d_%d" % (i, h), am[h % 2][:, :], tso, (128, 128),
                        rAM[h % 2])
                if hp == H - 2:
                    for half in range(2):
                        p.op("t", lambda eng, half=half: eng.matmul(
                            PA(3, half), orow,
                            poolC[0:1, 1024 + half * 512:1024 + (half + 1) * 512],
                            start=False, stop=True))
                tWo = p.dinc("t")
                rAM[0].r.append(tWo); rAM[1].r.append(tWo)
                p.wait("v", tod); p.use("v", rDBC)
                for h in (hp, hp + 1):
                    p.op("v", lambda eng, h=h: eng.scalar_tensor_tensor(
                        out=s8[:, h * 128:(h + 1) * 128],
                        in0=s8[:, h * 128:(h + 1) * 128],
                        scalar=smt[:, 24 + h:25 + h], in1=PA(0, h % 2, 0, 128),
                        op0=ALU.mult, op1=ALU.add))
                tvs = p.dinc("v")
                rPS[(0, 0)].r.append(tvs); rPS[(0, 1)].r.append(tvs)
                tvs_last[0] = tvs

            p.claim("t", rPS[(3, 0)], rPS[(3, 1)])
            emit_AT2(0)
            for hp in range(0, H, 2):
                emit_Am2(hp)
                if hp + 2 < H:
                    emit_AT2(hp + 2)
                emit_oD2(hp)
            rS8.w = tvs_last[0]
            tap("s8_%d" % i, s8[:, :], rS8.w, (128, 1024), rS8)
            two = p.dinc("t")
            rPS[(3, 0)].w = two; rPS[(3, 1)].w = two
            if _STAGE < 11:
                return
            p.wait("v", two); p.use("v", rXB[sl])
            p.op("v", lambda eng: eng.tensor_tensor(out=x2b[:, :],
                                                    in0=xb[sl][:, :],
                                                    in1=PT[3][:, :], op=ALU.add))
            rX2.w = p.dinc("v")
            rXB[sl].r.append(rX2.w)
            rPS[(3, 0)].r.append(rX2.w); rPS[(3, 1)].r.append(rX2.w)
            tap("x2_%d" % i, x2b[:, :], rX2.w, (128, 1024), rX2)
            p.wait("g", rX2.w)
            rX2.r.append(p.dma("g", x2_hbm[i * C:(i + 1) * C, :], x2b[:, :]))

            if _STAGE < 12:
                return
            p.use("v", rX2)
            p.op("v", lambda eng: eng.bn_stats(smt[:, 40:46], x2b[:, 0:512]))
            p.op("v", lambda eng: eng.bn_stats(smt[:, 46:52], x2b[:, 512:1024]))
            p.op("v", lambda eng: eng.drain())
            p.op("v", lambda eng: eng.bn_aggr(smt[:, 52:54], smt[:, 40:52]))
            p.op("v", lambda eng: eng.drain())
            p.op("v", lambda eng: eng.tensor_scalar(out=smt[:, 55:56],
                                                    in0=smt[:, 53:54],
                                                    scalar1=1.0 + LN_EPS,
                                                    scalar2=LN_EPS * LN_EPS,
                                                    op0=ALU.mult, op1=ALU.add))
            tv2 = p.dinc("v")
            p.wait("s", tv2)
            p.op("s", lambda eng: eng.activation(out=smt[:, 54:55], in_=smt[:, 55:56],
                                                 func=AF.Sqrt))
            ts2 = p.dinc("s")
            p.wait("v", ts2)
            p.op("v", lambda eng: eng.reciprocal(smt[:, 54:55], smt[:, 54:55]))
            p.op("v", lambda eng: eng.drain())
            p.op("v", lambda eng: eng.tensor_scalar(out=qtm(), in0=x2b[:, :],
                                                    scalar1=smt[:, 52:53],
                                                    scalar2=smt[:, 54:55],
                                                    op0=ALU.subtract, op1=ALU.mult))
            p.op("v", lambda eng: eng.tensor_tensor(out=qtm(), in0=qtm(),
                                                    in1=gbg(), op=ALU.mult))
            p.op("v", lambda eng: eng.tensor_tensor(out=qtm(), in0=qtm(),
                                                    in1=gbb(), op=ALU.add))
            tv4 = p.dinc("v")
            rSTAT.w = tv4
            tap("xn3_%d" % i, qtm(), tv4, (128, 1024), rQ)

            if _STAGE < 13:
                return
            p.wait("s", tv4)
            p.op("s", lambda eng: eng.activation(out=vtm(), in_=qtm(),
                                                 func=AF.Square,
                                                 accum_out=smt[:, 32:33]))
            p.op("s", lambda eng: eng.drain())
            p.op("s", lambda eng: eng.activation(out=smt[:, 33:34], in_=smt[:, 32:33],
                                                 func=AF.Sqrt))
            ts5 = p.dinc("s")
            p.wait("v", ts5)
            p.claim("v", rX2)
            p.op("v", lambda eng: eng.tensor_scalar(out=smt[:, 33:34], in0=smt[:, 33:34],
                                                    scalar1=NORM_EPS, scalar2=None,
                                                    op0=ALU.max))
            p.op("v", lambda eng: eng.reciprocal(smt[:, 34:35], smt[:, 33:34]))
            p.op("v", lambda eng: eng.drain())
            p.op("v", lambda eng: eng.tensor_tensor(out=smt[:, 34:35], in0=smt[:, 34:35],
                                                    in1=scl[:, :], op=ALU.mult))
            p.op("v", lambda eng: eng.tensor_scalar(out=khtm(), in0=qtm(),
                                                    scalar1=smt[:, 34:35], scalar2=None,
                                                    op0=ALU.mult))
            tv5 = p.dinc("v")
            tap("xq_%d" % i, khtm(), tv5, (128, 1024), rKH)

            if _STAGE < 14:
                return
            p.wait("t", tv5)
            p.claim("t", rPS[(1, 0)], rPS[(1, 1)])
            for j in range(8):
                half, qtr = (0, j) if j < 4 else (1, j - 4)
                p.op("t", lambda eng, j=j, half=half, qtr=qtr: eng.transpose(
                    PA(1, half, qtr * 128, (qtr + 1) * 128),
                    khtm(j * 128, (j + 1) * 128), idn[:, :]))
            tt2 = p.dinc("t")
            rPS[(1, 0)].w = tt2; rPS[(1, 1)].w = tt2
            p.wait("s", tt2)
            p.claim("s", rXNT)
            p.op("s", lambda eng: eng.copy(xnT(0, 512), PA(1, 0)))
            p.op("s", lambda eng: eng.copy(xnT(512, 1024), PA(1, 1)))
            ts6 = p.dinc("s")
            rXNT.w = ts6
            rPS[(1, 0)].r.append(ts6); rPS[(1, 1)].r.append(ts6)
            rX2.r.append(ts6)

            if _STAGE < 15:
                return
            p.use("t", rXNT, rMU)
            p.claim("t", rPS[(0, 0)], rPS[(0, 1)])
            for k in range(8):
                p.op("t", lambda eng, k=k: eng.matmul(
                    PA(0, 0, 0, 128), mu[:, k * 256: k * 256 + 128],
                    xnT(k * 128, (k + 1) * 128), start=(k == 0), stop=(k == 7)))
                p.op("t", lambda eng, k=k: eng.matmul(
                    PA(0, 1, 0, 128), mu[:, k * 256 + 128: (k + 1) * 256],
                    xnT(k * 128, (k + 1) * 128), start=(k == 0), stop=(k == 7)))
            tlg = p.dinc("t")
            rPS[(0, 0)].w = tlg; rPS[(0, 1)].w = tlg
            rMU.r.append(tlg)
            p.wait("s", tlg)
            p.claim("s", rLG)
            p.op("s", lambda eng: eng.copy(lgsb[:, 0:128], PA(0, 0, 0, 128)))
            p.op("s", lambda eng: eng.copy(lgsb[:, 128:256], PA(0, 1, 0, 128)))
            tlge = p.dinc("s")
            rLG.w = tlge
            rPS[(0, 0)].r.append(tlge); rPS[(0, 1)].r.append(tlge)
            tap("lg_%d" % i, lgsb[:, :], tlge, (128, 256), rLG)
            p.wait("g", tlge)
            rLG.r.append(p.dma("g", lg_hbm[0, :, i * C:(i + 1) * C], lgsb[:, 0:128]))
            rLG.r.append(p.dma("g", lg_hbm[1, :, i * C:(i + 1) * C], lgsb[:, 128:256]))

            if _STAGE < 16:
                return
            for j in range(2):
                p.use("v", rLG, rFMZ)
                p.op("v", lambda eng, j=j: eng.reduce_max(
                    smt[:, 35:36], lgsb[:, j * 128:(j + 1) * 128], axis=AXL.X))
                p.op("v", lambda eng: eng.drain())
                p.op("v", lambda eng, j=j: eng.tensor_tensor(
                    out=smt[:, 36:37], in0=fm[:, j:j + 1], in1=smt[:, 35:36], op=ALU.max))
                p.op("v", lambda eng, j=j: eng.tensor_tensor(
                    out=smt[:, 37:38], in0=fm[:, j:j + 1], in1=smt[:, 36:37],
                    op=ALU.subtract))
                p.op("v", lambda eng, j=j: eng.tensor_copy(fm[:, j:j + 1],
                                                           smt[:, 36:37]))
                p.op("v", lambda eng: eng.tensor_scalar(
                    out=smt[:, 38:39], in0=smt[:, 36:37], scalar1=-1.0, scalar2=None,
                    op0=ALU.mult))
                tf1 = p.dinc("v")
                tap("fm_%d_%d" % (i, j), fm[:, :], tf1, (128, 2), rFMZ)
                p.wait("s", tf1)
                p.claim("s", rPSB)
                p.op("s", lambda eng: eng.activation(out=smt[:, 39:40],
                                                     in_=smt[:, 37:38], func=AF.Exp))
                p.op("s", lambda eng, j=j: eng.activation(
                    out=psb[:, 0:128],
                    in_=lgsb[:, j * 128:(j + 1) * 128],
                    func=AF.Exp, bias=smt[:, 38:39], scale=1.0,
                    accum_out=smt[:, 35:36]))
                tf2 = p.dinc("s")
                rPSB.w = tf2
                p.wait("v", tf2)
                p.op("v", lambda eng, j=j: eng.scalar_tensor_tensor(
                    out=fz[:, j:j + 1], in0=fz[:, j:j + 1], scalar=smt[:, 39:40],
                    in1=smt[:, 35:36], op0=ALU.mult, op1=ALU.add))
                tf3 = p.dinc("v")
                rFMZ.w = tf3
                tap("fz_%d_%d" % (i, j), fz[:, :], tf3, (128, 2), rFMZ)
                bk = (2, j)
                p.wait("t", tf2)
                p.claim("t", rPS[bk])
                p.op("t", lambda eng, j=j, bk=bk: eng.transpose(
                    PA(bk[0], bk[1], 0, 128), psb[:, 0:128],
                    idn[:, :]))
                tf4 = p.dinc("t")
                rPS[bk].w = tf4
                p.wait("s", tf4)
                if j == 0:
                    p.claim("s", rLG)
                p.op("s", lambda eng, j=j, bk=bk: eng.copy(
                    lgsb[:, j * 128:(j + 1) * 128], PA(bk[0], bk[1], 0, 128)))
                tf5 = p.dinc("s")
                tap("pt_%d_%d" % (i, j), lgsb[:, j * 128:(j + 1) * 128], tf5,
                    (128, 128), rLG)
                rPTM.w = tf5
                rPS[bk].r.append(tf5)
                rPSB.r.append(tf5)
                pd = 3 if j == 0 else 1
                p.wait("t", tf5)
                p.claim("t", rPS[(pd, 0)], rPS[(pd, 1)])
                for half in range(2):
                    p.op("t", lambda eng, j=j, pd=pd, half=half: eng.matmul(
                        PA(pd, half), lgsb[:, j * 128:(j + 1) * 128],
                        qtm(half * 512, (half + 1) * 512), start=True, stop=True))
                tf6 = p.dinc("t")
                rPS[(pd, 0)].w = tf6; rPS[(pd, 1)].w = tf6
                p.wait("v", tf6); p.use("v", rAACC)
                p.op("v", lambda eng, j=j, pd=pd: eng.scalar_tensor_tensor(
                    out=aacc[:, j * 1024:(j + 1) * 1024],
                    in0=aacc[:, j * 1024:(j + 1) * 1024],
                    scalar=smt[:, 39:40], in1=PT[pd][:, :], op0=ALU.mult, op1=ALU.add))
                tf7 = p.dinc("v")
                rAACC.w = tf7
                tap("aac_%d_%d" % (i, j), aacc[:, j * 1024:(j + 1) * 1024], tf7,
                    (128, 1024), rAACC)
                rPS[(pd, 0)].r.append(tf7); rPS[(pd, 1)].r.append(tf7)
                rLG.r.append(tf6)

            endtok = rAACC.w
            for r_ in (rQ, rK, rKH, rV, rQHT, rKHT):
                r_.r.append(endtok)

        for i in range(NCH if _NCHA is None else _NCHA):
            chunk_A(i)

        endA = [("sT", p.cnt["sT"]), ("sV", p.cnt["sV"]), ("sS", p.cnt["sS"])]
        for r_ in (rW1B[0], rW1B[1], rW2B[0], rW2B[1], rSOT, rMU):
            r_.r = r_.r + list(endA)
        p.claim("g", rMU)
        tb2r = p.dma("g", mu[:, 0:2048], b2r_h)
        rMU.w = tb2r

        if not _DO_B:
            p.wait("y", ("dG", p.cnt["dG"]))
            p.wait("y", ("dY", p.cnt["dY"]))
            return _realize(nc, stk, p)
        p.use("v", rAACC, rFMZ)
        p.claim("v", rAACC, rFMZ)
        p.op("v", lambda eng: eng.reciprocal(smt[:, 32:34], fz[:, :]))
        p.op("v", lambda eng: eng.drain())
        for j in range(2):
            p.op("v", lambda eng, j=j: eng.tensor_scalar(
                out=aacc[:, j * 1024:(j + 1) * 1024],
                in0=aacc[:, j * 1024:(j + 1) * 1024],
                scalar1=sm1[:, j:j + 1], scalar2=None, op0=ALU.mult))
        tB1 = p.dinc("v")
        rAACC.w = tB1
        tap("slotin", aacc[:, :], tB1, (128, 2048), rAACC)

        p.wait("t", tB1)
        p.claim("t", *[rPS[(b_, hf)] for b_ in range(4) for hf in range(2)])
        for q in range(16):
            k, j = q // 2, q % 2
            b_, hf, qt = q // 4, (q % 4) // 2, q % 2
            p.op("t", lambda eng, k=k, j=j, b_=b_, hf=hf, qt=qt: eng.transpose(
                PA(b_, hf, qt * 128, (qt + 1) * 128),
                aacc[:, j * 1024 + k * 128: j * 1024 + (k + 1) * 128], idn[:, :]))
        tB2 = p.dinc("t")
        for b_ in range(4):
            rPS[(b_, 0)].w = tB2; rPS[(b_, 1)].w = tB2
        p.wait("s", tB2)
        p.claim("s", rSIT)
        for b_ in range(4):
            for hf in range(2):
                p.op("s", lambda eng, b_=b_, hf=hf: eng.copy(
                    siT(b_ * 512 + hf * 256, b_ * 512 + hf * 256 + 256),
                    PA(b_, hf, 0, 256)))
        tB3 = p.dinc("s")
        rSIT.w = tB3
        for b_ in range(4):
            rPS[(b_, 0)].r.append(tB3); rPS[(b_, 1)].r.append(tB3)
        rAACC.r.append(tB3)
        tap("siT", siT(), tB3, (128, 2048), rSIT)

        for e in range(E):
            be = e % 2
            p.claim("g", rW1B[be], rW2B[be])
            tw1 = p.dma("g", w1b[be]().rearrange("p (k h) -> p k h", k=8),
                        w1_h[e, :, :].rearrange("(k p) h -> p k h", p=128))
            tw2 = p.dma("g", w2b[be]().rearrange("p (k d) -> p k d", k=2),
                        w2_h[e, :, :].rearrange("(k p) d -> p k d", p=128))
            rW1B[be].w = tw1; rW2B[be].w = tw2

            pb = e % 2
            p.waits("t", [tw1, tB3]); p.use("t", rSIT)
            p.claim("t", rPS[(pb, 0)], rPS[(pb, 1)])
            for m in range(2):
                for k in range(8):
                    p.op("t", lambda eng, e=e, m=m, k=k, pb=pb: eng.matmul(
                        PA(pb, m, 0, 16),
                        w1b[e % 2](k * 256 + m * 128, k * 256 + (m + 1) * 128),
                        siT(k * 256 + e * 16, k * 256 + (e + 1) * 16),
                        start=(k == 0), stop=(k == 7)))
            tBh = p.dinc("t")
            rPS[(pb, 0)].w = tBh; rPS[(pb, 1)].w = tBh
            rW1B[be].r.append(tBh); rSIT.r.append(tBh)
            p.wait("s", tBh)
            p.claim("s", rHTS)
            for m in range(2):
                p.op("s", lambda eng, e=e, m=m, pb=pb: eng.activation(
                    out=smt[:, 64 + m * 16:64 + (m + 1) * 16], in_=PA(pb, m, 0, 16),
                    func=AF.Gelu, bias=b1c[:, m * 16 + e: m * 16 + e + 1],
                    scale=1.0))
            tBg = p.dinc("s")
            rHTS.w = tBg
            rPS[(pb, 0)].r.append(tBg); rPS[(pb, 1)].r.append(tBg)

            po = 2 + e % 2
            p.waits("t", [tBg, tw2])
            p.claim("t", rPS[(po, 0)], rPS[(po, 1)])
            for m in range(2):
                for half in range(2):
                    p.op("t", lambda eng, e=e, m=m, half=half, po=po: eng.matmul(
                        PA(po, half, 0, 512, 0, 16), smt[:, 64 + m * 16:64 + (m + 1) * 16],
                        w2b[e % 2](m * 1024 + half * 512, m * 1024 + (half + 1) * 512),
                        start=(m == 0), stop=(m == 1)))
            tBs = p.dinc("t")
            rPS[(po, 0)].w = tBs; rPS[(po, 1)].w = tBs
            rW2B[be].r.append(tBs); rHTS.r.append(tBs)
            p.wait("s", tBs)
            p.claim("s", rSOT)
            p.op("s", lambda eng, po=po: eng.copy(x2b[0:16, 0:1024],
                                                  PT[po][0:16, :]))
            tBt = p.dinc("s")
            rSOT.w = tBt
            rPS[(po, 0)].r.append(tBt); rPS[(po, 1)].r.append(tBt)
            p.wait("g", tBt)
            dst = aacc[(e % 8) * 16:(e % 8) * 16 + 16,
                       (e // 8) * 1024:(e // 8) * 1024 + 1024]
            rSOT.r.append(p.dma("g", dst, x2b[0:16, 0:1024]))
        tok_so = ("dG", p.cnt["dG"])
        p.wait("v", tok_so)
        p.use("v", rMU)
        for j in range(2):
            p.op("v", lambda eng, j=j: eng.tensor_tensor(
                out=so[j](), in0=so[j](), in1=mu[:, j * 1024:(j + 1) * 1024],
                op=ALU.add))
        tok_so_v = p.dinc("v")
        rAACC.w = tok_so_v
        tap("slotout", aacc[:, :], tok_so_v, (128, 2048), rAACC)

        if not _DO_C:
            p.wait("y", ("dG", p.cnt["dG"]))
            p.wait("y", ("dY", p.cnt["dY"]))
            return _realize(nc, stk, p)
        def chunk_C(i):
            sl = i % 2
            p.wait("y", tok_so)
            p.claim("y", rX2C[sl], rW2B[0], rW2B[1])
            rX2C[sl].w = p.dma("y", x2c[sl](), x2_hbm[i * C:(i + 1) * C, :])
            p.claim("y", rLGC[sl])
            p.dma("y", lgc[sl](0, 128), lg_hbm[0, :, i * C:(i + 1) * C])
            rLGC[sl].w = p.dma("y", lgc[sl](128, 256), lg_hbm[1, :, i * C:(i + 1) * C])

            b_ = sl
            p.use("t", rLGC[sl])
            p.claim("t", rPS[(b_, 0)])
            p.op("t", lambda eng, sl=sl, b_=b_: eng.transpose(
                PA(b_, 0, 0, 128), lgc[sl](0, 128), idn[:, :]))
            p.op("t", lambda eng, sl=sl, b_=b_: eng.transpose(
                PA(b_, 0, 128, 256), lgc[sl](128, 256), idn[:, :]))
            tc1 = p.dinc("t")
            rPS[(b_, 0)].w = tc1
            p.wait("s", tc1)
            p.claim("s", rLTM, rSIT)
            p.op("s", lambda eng, b_=b_: eng.copy(ltm(0, 256), PA(b_, 0, 0, 256)))
            tc2 = p.dinc("s")
            rLTM.w = tc2
            rPS[(b_, 0)].r.append(tc2)
            rLGC[sl].r.append(tc2)
            p.use("v", rLTM)
            p.op("v", lambda eng: eng.reduce_max(smt[:, 32:33], ltm(0, 256), axis=AXL.X))
            p.op("v", lambda eng: eng.drain())
            p.op("v", lambda eng: eng.tensor_scalar(out=smt[:, 33:34], in0=smt[:, 32:33],
                                                    scalar1=-1.0, scalar2=None,
                                                    op0=ALU.mult))
            tc3 = p.dinc("v")
            p.wait("s", tc3)
            p.claim("s", rPTC)
            p.op("s", lambda eng: eng.activation(out=ptc(0, 256), in_=ltm(0, 256),
                                                 func=AF.Exp, bias=smt[:, 33:34],
                                                 scale=1.0, accum_out=smt[:, 34:35]))
            tc4 = p.dinc("s")
            rPTC.w = tc4
            rLTM.r.append(tc4)
            p.wait("v", tc4)
            p.op("v", lambda eng: eng.reciprocal(smt[:, 35:36], smt[:, 34:35]))
            p.op("v", lambda eng: eng.drain())
            p.op("v", lambda eng: eng.tensor_scalar(out=ptc(0, 256), in0=ptc(0, 256),
                                                    scalar1=smt[:, 35:36], scalar2=None,
                                                    op0=ALU.mult))
            tc5 = p.dinc("v")
            rPTC.w = tc5
            p.wait("t", tc5)
            p.claim("t", rPS[(b_, 1)])
            p.op("t", lambda eng, b_=b_: eng.transpose(PA(b_, 1, 0, 128),
                                                       ptc(0, 128), idn[:, :]))
            p.op("t", lambda eng, b_=b_: eng.transpose(PA(b_, 1, 128, 256),
                                                       ptc(128, 256), idn[:, :]))
            tc6 = p.dinc("t")
            rPS[(b_, 1)].w = tc6
            rPTC.r.append(tc6)
            p.wait("s", tc6)
            p.claim("s", rCT)
            p.op("s", lambda eng, b_=b_: eng.copy(ct(0, 256), PA(b_, 1, 0, 256)))
            tc7 = p.dinc("s")
            rCT.w = tc7
            rPS[(b_, 1)].r.append(tc7)
            po = 2 + sl
            p.use("t", rCT, rAACC)
            p.wait("t", tok_so)
            p.claim("t", rPS[(po, 0)], rPS[(po, 1)])
            for j in range(2):
                for half in range(2):
                    p.op("t", lambda eng, j=j, half=half, po=po: eng.matmul(
                        PA(po, half), ct(j * 128, (j + 1) * 128),
                        so[j](half * 512, (half + 1) * 512),
                        start=(j == 0), stop=(j == 1)))
            tc8 = p.dinc("t")
            rPS[(po, 0)].w = tc8; rPS[(po, 1)].w = tc8
            rCT.r.append(tc8)
            p.wait("v", tc8); p.use("v", rX2C[sl])
            p.claim("v", rOUTC[sl])
            p.op("v", lambda eng, sl=sl, po=po: eng.tensor_tensor(
                out=outc[sl](), in0=x2c[sl](), in1=PT[po][:, :], op=ALU.add))
            tc9 = p.dinc("v")
            rOUTC[sl].w = tc9
            rX2C[sl].r.append(tc9)
            rPS[(po, 0)].r.append(tc9); rPS[(po, 1)].r.append(tc9)
            p.wait("g", tc9)
            rOUTC[sl].r.append(p.dma("g", y_h[i * C:(i + 1) * C, :], outc[sl]()))

        for i in range(NCH):
            chunk_C(i)

        p.wait("y", ("dG", p.cnt["dG"]))
        p.wait("y", ("dY", p.cnt["dY"]))
        return _realize(nc, stk, p)


def _realize(nc, stk, p):
    block = stk.enter_context(nc.Block())

    @block.tensor
    def _(eng):
        for fn in p.stream["t"]:
            fn(eng)

    @block.vector
    def _(eng):
        for fn in p.stream["v"]:
            fn(eng)

    @block.scalar
    def _(eng):
        for fn in p.stream["s"]:
            fn(eng)

    @block.gpsimd
    def _(eng):
        for fn in p.stream["g"]:
            fn(eng)

    @block.sync
    def _(eng):
        for fn in p.stream["y"]:
            fn(eng)
    return nc


def _host_consts():
    import jax
    import jax.numpy as jnp
    half = DH // 2
    with jax.default_device(jax.devices("cpu")[0]):
        inv = 1.0 / ROPE_BASE ** (jnp.arange(half, dtype=jnp.float32) / half)
        freqs = jnp.arange(L, dtype=jnp.float32)[:, None] * inv[None, :]
        cos = np.asarray(jnp.cos(freqs), np.float32)
        sin = np.asarray(jnp.sin(freqs), np.float32)
    costab = np.empty((L, DH), np.float32)
    sintab = np.empty((L, DH), np.float32)
    costab[:, 0::2] = cos; costab[:, 1::2] = cos
    sintab[:, 0::2] = -sin; sintab[:, 1::2] = sin
    s_idx = np.arange(128)[:, None]
    t_idx = np.arange(128)[None, :]
    cstv = np.zeros((128, 4), np.float32)
    cstv[:, 0] = LN_EPS
    return {
        "costab": costab,
        "sintab": sintab,
        "identity": np.eye(128, dtype=np.float32),
        "maskut": (s_idx <= t_idx).astype(np.float32),
        "onescol": np.ones((128, 1), np.float32),
        "ones8": np.ones((H, 128), np.float32),
        "cst": cstv,
    }


def _host_inputs(L0_attn_x, L0_attn_Wq, L0_attn_Wk, L0_attn_Wv, L0_attn_Wbeta,
                 L0_attn_bbeta, L0_attn_Walpha, L0_attn_balpha, L0_attn_Wo,
                 L0_attn_bo, L0_ffn_ln_gamma, L0_ffn_ln_beta, L0_ffn_mu,
                 L0_ffn_scale, L0_ffn_w1, L0_ffn_b1, L0_ffn_w2, L0_ffn_b2):
    f = np.float32
    x = np.ascontiguousarray(np.asarray(L0_attn_x, f))
    wab = np.concatenate([np.asarray(L0_attn_Wbeta, f),
                          np.asarray(L0_attn_Walpha, f)], 0)
    b1 = np.asarray(L0_ffn_b1, f)
    b1cols = np.zeros((128, 32), f)
    for m in range(2):
        for e in range(E):
            b1cols[:, m * 16 + e] = b1[e, m * 128:(m + 1) * 128]
    shared = {
        "wqT": np.ascontiguousarray(np.asarray(L0_attn_Wq, f).T),
        "wkT": np.ascontiguousarray(np.asarray(L0_attn_Wk, f).T),
        "wvT": np.ascontiguousarray(np.asarray(L0_attn_Wv, f).T),
        "woT": np.ascontiguousarray(np.asarray(L0_attn_Wo, f).T),
        "wabT": np.ascontiguousarray(wab.T),
        "boflat": np.ascontiguousarray(np.asarray(L0_attn_bo, f).reshape(1, D)),
        "muflat": np.ascontiguousarray(np.asarray(L0_ffn_mu, f).reshape(D, ES)),
        "gbg": np.ascontiguousarray(
            np.broadcast_to(np.asarray(L0_ffn_ln_gamma, f), (128, D))),
        "gbb": np.ascontiguousarray(
            np.broadcast_to(np.asarray(L0_ffn_ln_beta, f), (128, D))),
        "sclbc": np.ascontiguousarray(np.broadcast_to(
            np.asarray(L0_ffn_scale, f).reshape(1, 1), (128, 1))),
        "b1cols": b1cols,
        "b2rep": np.ascontiguousarray(np.concatenate(
            [np.repeat(np.asarray(L0_ffn_b2, f), NS, axis=0)[0:128],
             np.repeat(np.asarray(L0_ffn_b2, f), NS, axis=0)[128:256]], axis=1)),
        "w1": np.ascontiguousarray(np.asarray(L0_ffn_w1, f)),
        "w2": np.ascontiguousarray(np.asarray(L0_ffn_w2, f)),
    }
    shared.update(_host_consts())
    cstv = shared["cst"].copy()
    cstv[0:8, 1] = np.asarray(L0_attn_bbeta, f)
    cstv[0:8, 2] = np.asarray(L0_attn_balpha, f)
    shared["cst"] = cstv
    in_maps = []
    for b in range(B):
        m = dict(shared)
        m["x"] = np.ascontiguousarray(x[b])
        in_maps.append(m)
    return in_maps


_NC = None


def kernel(**inputs):
    global _NC
    if _NC is None:
        _NC = _build()
    in_maps = _host_inputs(**inputs)
    res = run_bass_kernel_spmd(_NC, in_maps, core_ids=list(range(B)))
    out = np.stack([np.asarray(res.results[b]["y"]) for b in range(B)], axis=0)
    return np.ascontiguousarray(out.astype(np.float32))
